# version 07
# baseline (speedup 1.0000x reference)
import numpy as np
import concourse.bass as bass
import concourse.mybir as mybir
from concourse.bass_utils import run_bass_kernel_spmd

F32 = mybir.dt.float32
BF16 = mybir.dt.bfloat16
AF = mybir.ActivationFunctionType
ALU = mybir.AluOpType
AX = mybir.AxisListType

D = 2048
T = 2048
NK = 16
NE = 16
HFT = 1024
LN_EPS = 1e-5
MOE_EPS = 1e-9
RATIO = 0.1

DEBUG = False
SKIP_EXPERTS = 0

ENGS = ("tensor", "vector", "scalar", "gpsimd", "sync")


class Prog:
    def __init__(self):
        self.ops = {e: [] for e in ENGS}
        self.sems = {}
        self.cnt = {}

    def add_sem(self, name, handle):
        self.sems[name] = handle
        self.cnt[name] = 0

    def op(self, eng, fn, inc=None, amt=1):
        if inc is None:
            self.ops[eng].append((fn, None, 0, ("op", None, 0)))
            return None
        self.cnt[inc] += amt
        self.ops[eng].append((fn, self.sems[inc], amt, ("inc", inc, amt)))
        return self.cnt[inc]

    def wait(self, eng, sem, val):
        if val is None or val <= 0:
            return
        h = self.sems[sem]
        self.ops[eng].append((lambda e, h=h, v=val: e.wait_ge(h, v), None, 0,
                              ("wait", sem, val)))

    def replay(self, eng_obj, eng_name):
        for fn, semh, amt, _meta in self.ops[eng_name]:
            instr = fn(eng_obj)
            if semh is not None:
                instr.then_inc(semh, amt)

    def check(self):
        counts = {s: 0 for s in self.cnt}
        ptrs = {e: 0 for e in self.ops}
        total = sum(len(v) for v in self.ops.values())
        done = 0
        while done < total:
            progress = False
            for e in self.ops:
                lst = self.ops[e]
                while ptrs[e] < len(lst):
                    _fn, _h, _amt, meta = lst[ptrs[e]]
                    if meta[0] == "wait" and counts[meta[1]] < meta[2]:
                        break
                    if meta[0] == "inc":
                        counts[meta[1]] += meta[2]
                    ptrs[e] += 1
                    done += 1
                    progress = True
            if not progress:
                blocked = {e: (ptrs[e], self.ops[e][ptrs[e]][3] if ptrs[e] < len(self.ops[e]) else None)
                           for e in self.ops if ptrs[e] < len(self.ops[e])}
                raise RuntimeError(f"DEADLOCK: counts={counts} blocked={blocked}")
        return True


def build_program(debug=False):
    nc = bass.Bass("TRN2", target_bir_lowering=False)

    dr = {}

    def din(name, shape, dt=F32):
        dr[name] = nc.dram_tensor(name, shape, dt, kind="ExternalInput").ap()

    din("xT_r", [128, NK * T])
    din("winT_r", [128, NK * D])
    din("woutT_q", [4, 128, NK * 512])
    din("wdtT_r", [128, NK * 32])
    din("centT_r", [128, NK * 16])
    din("logA_r", [128, 16])
    din("bdt_c", [32, 1])
    din("gb_row", [1, 16])
    din("wg_r", [NE, 128, NK * 1024])
    din("wu_r", [NE, 128, NK * 1024])
    din("wd_r", [NE, 128, 8 * D])
    din("wgs_r", [2, 128, NK * 1024])
    din("wus_r", [2, 128, NK * 1024])
    din("wds_r", [2, 128, 8 * D])
    din("SEL", [32, D])
    din("SELB", [16, D])
    din("EYE", [128, 128])
    din("ONESCOL", [128, 1])
    din("ONES1", [1, 128])

    outT = nc.dram_tensor("outT", [D, T], F32, kind="ExternalOutput").ap()
    if debug:
        dbg_xp = nc.dram_tensor("dbg_xp", [128, NK * T], F32, kind="ExternalOutput").ap()
        dbg_ys = nc.dram_tensor("dbg_ys", [128, NK * T], F32, kind="ExternalOutput").ap()
        dbg_x1 = nc.dram_tensor("dbg_x1", [128, NK * T], F32, kind="ExternalOutput").ap()
        dbg_mask = nc.dram_tensor("dbg_mask", [16, T], F32, kind="ExternalOutput").ap()
        dbg_dt = nc.dram_tensor("dbg_dt", [32, T], F32, kind="ExternalOutput").ap()
        dbg_aff = nc.dram_tensor("dbg_aff", [128, 16], F32, kind="ExternalOutput").ap()
        dbg_btok = nc.dram_tensor("dbg_btok", [128, 16], F32, kind="ExternalOutput").ap()
        dbg_m1 = nc.dram_tensor("dbg_m1", [128, 1], F32, kind="ExternalOutput").ap()
        dbg_t1r = nc.dram_tensor("dbg_t1r", [128, 16], F32, kind="ExternalOutput").ap()
        dbg_mskt = nc.dram_tensor("dbg_mskt", [128, 16], F32, kind="ExternalOutput").ap()
        dbg_acc = nc.dram_tensor("dbg_acc", [128, 16384], F32, kind="ExternalOutput").ap()
        dbg_hb = nc.dram_tensor("dbg_hb", [128, 16384], F32, kind="ExternalOutput").ap()

    SB0 = ((nc.sbuf_base + 63) // 64) * 64

    def at(name, shape, dt, off):
        return nc.alloc_sbuf_tensor_at(name, shape, dt, offset=SB0 + off).ap()

    KB = 1024
    BIG1 = at("BIG1", [128, 32768], BF16, 0)
    BIG2 = at("BIG2", [128, 32768], BF16, 64 * KB)
    A0 = 128 * KB
    WIB = at("WIB", [128, 32768], BF16, A0)
    SELsb = at("SELsb", [32, 2048], BF16, A0)
    DTS = at("DTS", [32, 2048], BF16, A0 + 4 * KB)
    WDTb = at("WDTb", [128, 512], BF16, A0 + 8 * KB)
    XRB = at("XRB", [128, 2048], BF16, A0 + 12 * KB)
    ASB = [at("ASB0", [128, 2048], F32, A0 + 12 * KB),
           at("ASB1", [128, 2048], F32, A0 + 20 * KB)]
    ZSQ = [at("ZSQ0", [128, 2048], BF16, A0 + 16 * KB),
           at("ZSQ1", [128, 2048], BF16, A0 + 20 * KB)]
    T1 = at("T1", [128, 2048], BF16, A0 + 28 * KB)
    RBSB = at("RBSB", [128, 2048], BF16, A0 + 16 * KB)
    MRSB = at("MRSB", [128, 2048], BF16, A0 + 20 * KB)
    R1 = at("R1", [1, 2048], BF16, A0 + 24 * KB)
    R2 = at("R2", [1, 2048], BF16, A0 + 28 * KB)
    WOBQ = [at("WOBQ0", [128, 8192], BF16, A0 + 32 * KB),
            at("WOBQ1", [128, 8192], BF16, A0 + 48 * KB)]
    SLOT = [at("SLOTA", [128, 16384], BF16, A0),
            at("SLOTB", [128, 16384], BF16, A0 + 32 * KB)]
    B2 = 64 * KB
    ACC = at("ACC", [128, 16384], BF16, B2)
    HBUF = at("HBUF", [128, 8192], BF16, B2 + 32 * KB)
    AFFT = at("AFFT", [128, 256], F32, B2 + 48 * KB)
    BTOK = at("BTOK", [128, 256], F32, B2 + 49 * KB)
    B2R = at("B2R", [128, 256], F32, B2 + 50 * KB)
    T1R = at("T1R", [128, 256], F32, B2 + 51 * KB)
    T2R = at("T2R", [128, 256], F32, B2 + 62 * KB)
    WSEL = at("WSEL", [128, 256], F32, B2 + 63 * KB)
    XSTG = at("XSTG", [128, 16384], F32, 0)
    UM = [at("UM0", [128, 1024], BF16, B2 + 52 * KB),
          at("UM1", [128, 1024], BF16, B2 + 54 * KB)]
    MBSB = at("MBSB", [128, 1024], BF16, B2 + 56 * KB)
    OST = [at("OST0", [128, 1024], F32, B2 + 58 * KB),
           at("OST1", [128, 1024], F32, B2 + 44 * KB)]
    ZSQF = [at("ZSQF0", [128, 1024], BF16, B2 + 32 * KB),
            at("ZSQF1", [128, 1024], BF16, B2 + 34 * KB)]
    FR1 = at("FR1", [1, 1024], BF16, B2 + 36 * KB)
    FR2 = at("FR2", [1, 1024], BF16, B2 + 38 * KB)
    T1F = at("T1F", [128, 1024], F32, B2 + 40 * KB)

    S0 = 192 * KB
    SELBsb = at("SELBsb", [16, 2048], BF16, S0)
    MASKF = at("MASKF", [16, 2048], BF16, S0 + 4 * KB)
    CTBb = at("CTBb", [128, 256], BF16, S0 + 8 * KB)
    EYEf = at("EYEf", [128, 128], F32, S0 + 8 * KB + 512)
    ONESCOLb = at("ONESCOLb", [128, 1], BF16, S0 + 9 * KB)
    ONES1b = at("ONES1b", [1, 128], BF16, S0 + 9 * KB + 64)
    ONES1f = at("ONES1f", [1, 128], F32, S0 + 9 * KB + 384)
    LOGR = at("LOGR", [128, 16], F32, S0 + 10 * KB)
    ACsb = at("ACsb", [128, 16], F32, S0 + 10 * KB + 64)
    BDT = at("BDT", [32, 1], F32, S0 + 10 * KB + 128)
    GBROW = at("GBROW", [1, 16], F32, S0 + 10 * KB + 192)
    GBBC = at("GBBC", [128, 16], F32, S0 + 10 * KB + 256)
    MSKT = at("MSKT", [128, 256], F32, S0 + 10 * KB + 320)
    M1R = at("M1R", [128, 16], F32, S0 + 11 * KB + 384)
    DENR = at("DENR", [128, 16], F32, S0 + 11 * KB + 448)
    DINV = at("DINV", [128, 16], F32, S0 + 11 * KB + 512)

    P = Prog()

    import contextlib
    stack = contextlib.ExitStack()
    with stack:
        PS = stack.enter_context(nc.psum_tensor("PS", [128, 4096], F32))
        for s in ("qg", "qs", "pe", "act", "dve"):
            P.add_sem(s, stack.enter_context(nc.semaphore(s)))

        def gdma(dst, src):
            return P.op("gpsimd", lambda g, d=dst, s=src: g.dma_start(out=d, in_=s),
                        inc="qg", amt=16)

        def sdma(dst, src):
            return P.op("sync", lambda g, d=dst, s=src: g.dma_start(out=d, in_=s),
                        inc="qs", amt=16)

        def mm(out, lhsT, rhs, start, stop, inc=None):
            return P.op("tensor",
                        lambda t, o=out, l=lhsT, r=rhs, st=start, sp=stop:
                        t.matmul(o, l, r, start=st, stop=sp),
                        inc=inc)

        q_const = []
        q_const.append(gdma(SELBsb[:, :], dr["SELB"]))
        q_const.append(gdma(CTBb[:, :], dr["centT_r"]))
        q_const.append(gdma(EYEf[:, :], dr["EYE"]))
        q_const.append(gdma(ONESCOLb[:, :], dr["ONESCOL"]))
        q_const.append(gdma(ONES1b[:, :], dr["ONES1"]))
        q_const.append(gdma(ONES1f[:, :], dr["ONES1"]))
        q_const.append(gdma(LOGR[:, :], dr["logA_r"]))
        q_const.append(gdma(BDT[:, :], dr["bdt_c"]))
        q_const.append(gdma(GBROW[:, :], dr["gb_row"]))
        q_xb = gdma(BIG2[:, 0:16384], dr["xT_r"][:, 0:16384])
        q_wib = gdma(WIB[:, 0:16384], dr["winT_r"][:, 0:16384])
        q_x2 = sdma(XSTG[:, :], dr["xT_r"][:, 16384:32768])
        P.wait("scalar", "qs", q_x2)
        c_x2 = P.op("scalar", lambda s: s.copy(BIG2[:, 16384:32768], XSTG[:, :]), inc="act")
        P.wait("sync", "act", c_x2)
        q_w2 = sdma(XSTG[:, :], dr["winT_r"][:, 16384:32768])
        P.wait("scalar", "qs", q_w2)
        c_w2 = P.op("scalar", lambda s: s.copy(WIB[:, 16384:32768], XSTG[:, :]), inc="act")

        P.wait("scalar", "qg", q_const[8])
        c_exp = P.op("scalar", lambda s: s.activation(ACsb[:, :], LOGR[:, :], AF.Exp),
                     inc="act")
        P.wait("vector", "act", c_exp)
        c_neg = P.op("vector",
                     lambda v: v.tensor_scalar(ACsb[:, :], ACsb[:, :], -1.0, None, ALU.mult),
                     inc="dve")
        P.wait("tensor", "qg", q_const[8])
        c_gb = mm(PS[:, 3584:3600], ONES1f[:, :], GBROW[:, :], True, True, inc="pe")
        P.wait("vector", "pe", c_gb)
        P.op("vector", lambda v: v.tensor_copy(GBBC[:, :], PS[:, 3584:3600]), inc="dve")

        xp_ev = {}
        P.wait("tensor", "qg", q_wib)
        P.wait("tensor", "act", c_w2)
        for e in range(NK):
            pb = (e % 2) * 2048
            if e >= 2:
                P.wait("tensor", "act", xp_ev[e - 2])
            xp_mm = None
            for k in range(NK):
                for tc in range(4):
                    xp_mm = mm(PS[:, pb + tc * 512: pb + (tc + 1) * 512],
                               WIB[:, k * 2048 + e * 128: k * 2048 + (e + 1) * 128],
                               BIG2[:, k * 2048 + tc * 512: k * 2048 + (tc + 1) * 512],
                               k == 0, k == NK - 1,
                               inc="pe" if (k == NK - 1 and tc == 3) else None)
            P.wait("scalar", "pe", xp_mm)
            xp_ev[e] = P.op("scalar",
                            lambda s, e=e, pb=pb: s.copy(BIG1[:, e * 2048:(e + 1) * 2048],
                                                         PS[:, pb:pb + 2048]),
                            inc="act")
        xp_done_pe = P.cnt["pe"]
        xp_ev_all = xp_ev[NK - 1]

        if debug:
            P.wait("gpsimd", "act", xp_ev_all)
            q_dbg_xp = gdma(dbg_xp, BIG1[:, :])

        P.wait("gpsimd", "pe", xp_done_pe)
        q_sel = gdma(SELsb[:, :], dr["SEL"])
        q_wdt = gdma(WDTb[:, :], dr["wdtT_r"])
        q_wob = [gdma(WOBQ[0][:, :], dr["woutT_q"][0]),
                 gdma(WOBQ[1][:, :], dr["woutT_q"][1])]

        P.wait("tensor", "qg", q_wdt)
        P.wait("tensor", "act", xp_ev_all)
        dt_mm = None
        for k in range(NK):
            for tc in range(4):
                dt_mm = mm(PS[0:32, tc * 512:(tc + 1) * 512],
                           WDTb[:, k * 32:(k + 1) * 32],
                           BIG1[:, k * 2048 + tc * 512: k * 2048 + (tc + 1) * 512],
                           k == 0, k == NK - 1,
                           inc="pe" if (k == NK - 1 and tc == 3) else None)
        P.wait("scalar", "pe", dt_mm)
        P.op("scalar", lambda s: s.activation(PS[0:32, 2048:4096], PS[0:32, 0:2048],
                                              AF.Exp, bias=BDT[:, 0:1]), inc="act")
        dts_done = P.op("scalar", lambda s: s.activation(DTS[:, :], PS[0:32, 2048:4096],
                                                         AF.Ln, bias=1.0), inc="act")
        if debug:
            P.wait("gpsimd", "act", dts_done)
            gdma(dbg_dt, DTS[:, :])

        scan_done = {}
        aexp = {}
        P.wait("tensor", "act", dts_done)
        P.wait("tensor", "qg", q_sel)
        P.wait("scalar", "dve", c_neg)
        for ct in range(NK):
            if ct >= 1:
                P.wait("tensor", "act", aexp[ct - 1])
            sel_mm = None
            for tc in range(4):
                sel_mm = mm(PS[:, 2048 + tc * 512: 2048 + (tc + 1) * 512],
                            SELsb[:, ct * 128:(ct + 1) * 128],
                            DTS[:, tc * 512:(tc + 1) * 512],
                            True, True, inc="pe" if tc == 3 else None)
            P.wait("scalar", "pe", sel_mm)
            if ct >= 2:
                P.wait("scalar", "dve", scan_done[ct - 2])
            aexp[ct] = P.op("scalar",
                            lambda s, ct=ct: s.activation(ASB[ct % 2][:, :], PS[:, 2048:4096],
                                                          AF.Exp, scale=ACsb[:, ct:ct + 1]),
                            inc="act")
            P.wait("vector", "act", aexp[ct])
            scan_done[ct] = P.op("vector",
                                 lambda v, ct=ct: v.tensor_tensor_scan(
                                     BIG2[:, ct * 2048:(ct + 1) * 2048],
                                     ASB[ct % 2][:, :],
                                     BIG1[:, ct * 2048:(ct + 1) * 2048],
                                     0.0, ALU.mult, ALU.add),
                                 inc="dve")
            if ct == 0:
                P.wait("tensor", "act", dts_done)
                P.wait("tensor", "qg", q_wob[0])
            for kc in ([ct - 1] if ct >= 1 else []) + ([NK - 1] if ct == NK - 1 else []):
                P.wait("tensor", "dve", scan_done[kc])
                for tc in range(4):
                    y0_mm = mm(PS[:, tc * 512:(tc + 1) * 512],
                               WOBQ[0][:, kc * 512: kc * 512 + 128],
                               BIG2[:, kc * 2048 + tc * 512: kc * 2048 + (tc + 1) * 512],
                               kc == 0, kc == NK - 1,
                               inc="pe" if (kc == NK - 1 and tc == 3) else None)
        if debug:
            P.wait("gpsimd", "dve", scan_done[NK - 1])
            q_dbg_ys = gdma(dbg_ys, BIG2[:, :])

        z_done = {}
        y_done = {}
        y_done[0] = y0_mm
        wobq_next = 2
        for e in range(NK):
            if e >= 1:
                P.wait("gpsimd", "dve", z_done[e - 1])
            else:
                P.wait("gpsimd", "dve", scan_done[NK - 1])
            xrb_q = gdma(XRB[:, :], dr["xT_r"][:, e * 2048:(e + 1) * 2048])
            q = e // 4
            pb = (e % 2) * 2048
            if e >= 1:
                P.wait("tensor", "qg", q_wob[q])
            if e == 1:
                P.wait("tensor", "act", aexp[NK - 1])
            if e >= 2:
                P.wait("tensor", "dve", z_done[e - 2])
            for k in range(NK):
                for tc in range(4):
                    if e == 0:
                        break
                    y_done[e] = mm(PS[:, pb + tc * 512: pb + (tc + 1) * 512],
                                   WOBQ[q % 2][:, k * 512 + (e % 4) * 128: k * 512 + (e % 4 + 1) * 128],
                                   BIG2[:, k * 2048 + tc * 512: k * 2048 + (tc + 1) * 512],
                                   k == 0, k == NK - 1,
                                   inc="pe" if (k == NK - 1 and tc == 3) else None)
            if e % 4 == 3 and wobq_next < 4:
                P.wait("gpsimd", "pe", y_done[e])
                q_wob.append(gdma(WOBQ[wobq_next % 2][:, :], dr["woutT_q"][wobq_next]))
                wobq_next += 1
            P.wait("vector", "pe", y_done[e])
            P.wait("vector", "qg", xrb_q)
            if debug and e == 0:
                P.wait("vector", "qg", q_dbg_xp)
            z_done[e] = P.op("vector",
                             lambda v, e=e, pb=pb: v.tensor_tensor(
                                 BIG1[:, e * 2048:(e + 1) * 2048],
                                 XRB[:, :], PS[:, pb:pb + 2048], ALU.add),
                             inc="dve")

        zq = {}
        stat_mm = {}
        for e in range(NK):
            P.wait("scalar", "dve", z_done[e])
            if e >= 2:
                P.wait("scalar", "pe", stat_mm[e - 2])
            zq[e] = P.op("scalar",
                         lambda s, e=e: s.activation(ZSQ[e % 2][:, :],
                                                     BIG1[:, e * 2048:(e + 1) * 2048],
                                                     AF.Square),
                         inc="act")
            if e == 0:
                P.wait("tensor", "dve", z_done[NK - 1])
                P.wait("tensor", "act", aexp[NK - 1])
            P.wait("tensor", "act", zq[e])
            for tc in range(4):
                mm(PS[0:1, tc * 512:(tc + 1) * 512], ONESCOLb[:, :],
                   BIG1[:, e * 2048 + tc * 512: e * 2048 + (tc + 1) * 512],
                   e == 0, e == NK - 1)
                stat_mm[e] = mm(PS[0:1, 2048 + tc * 512: 2048 + (tc + 1) * 512], ONESCOLb[:, :],
                                ZSQ[e % 2][:, tc * 512:(tc + 1) * 512],
                                e == 0, e == NK - 1,
                                inc="pe" if tc == 3 else None)
        P.wait("vector", "pe", stat_mm[NK - 1])
        def dvr(fn):
            c = P.op("vector", fn, inc="dve")
            P.wait("vector", "dve", c)
            return c
        dvr(lambda v: v.tensor_scalar(R1[:, :], PS[0:1, 0:2048], 1.0 / D, None, ALU.mult))
        dvr(lambda v: v.tensor_scalar(R2[:, :], PS[0:1, 2048:4096], 1.0 / D, None, ALU.mult))
        dvr(lambda v: v.tensor_tensor(PS[0:1, 0:2048], R1[:, :], R1[:, :], ALU.mult))
        dvr(lambda v: v.tensor_tensor(PS[0:1, 2048:4096], R2[:, :], PS[0:1, 0:2048], ALU.subtract))
        r_var = dvr(lambda v: v.tensor_scalar(PS[0:1, 2048:4096], PS[0:1, 2048:4096], LN_EPS, None, ALU.add))
        P.wait("scalar", "dve", r_var)
        r_ln = P.op("scalar", lambda s: s.activation(PS[0:1, 0:2048], PS[0:1, 2048:4096], AF.Ln), inc="act")
        P.wait("vector", "act", r_ln)
        r_lh = dvr(lambda v: v.tensor_scalar(PS[0:1, 0:2048], PS[0:1, 0:2048], -0.5, None, ALU.mult))
        P.wait("scalar", "dve", r_lh)
        r_rstd = P.op("scalar", lambda s: s.activation(R2[:, :], PS[0:1, 0:2048], AF.Exp), inc="act")
        P.wait("vector", "act", r_rstd)
        r_mr = dvr(lambda v: v.tensor_tensor(R1[:, :], R1[:, :], R2[:, :], ALU.mult))
        P.wait("tensor", "dve", r_mr)
        bc_mm = None
        for tc in range(4):
            mm(PS[:, tc * 512:(tc + 1) * 512], ONES1b[:, :],
               R2[0:1, tc * 512:(tc + 1) * 512], True, True)
            bc_mm = mm(PS[:, 2048 + tc * 512: 2048 + (tc + 1) * 512], ONES1b[:, :],
                       R1[0:1, tc * 512:(tc + 1) * 512], True, True,
                       inc="pe" if tc == 3 else None)
        mixer_pe_done = bc_mm
        P.wait("scalar", "pe", bc_mm)
        P.op("scalar", lambda s: s.copy(RBSB[:, :], PS[:, 0:2048]), inc="act")
        bc_ev = P.op("scalar", lambda s: s.copy(MRSB[:, :], PS[:, 2048:4096]), inc="act")
        ap_done = {}
        P.wait("vector", "act", bc_ev)
        for e in range(NK):
            P.op("vector", lambda v, e=e: v.tensor_tensor(
                T1[:, :], BIG1[:, e * 2048:(e + 1) * 2048], RBSB[:, :], ALU.mult),
                inc="dve")
            ap_done[e] = P.op("vector", lambda v, e=e: v.tensor_tensor(
                BIG1[:, e * 2048:(e + 1) * 2048], T1[:, :], MRSB[:, :], ALU.subtract),
                inc="dve")
        x1_done = ap_done[NK - 1]
        if debug:
            P.wait("gpsimd", "dve", x1_done)
            gdma(dbg_x1, BIG1[:, :])

        P.wait("tensor", "dve", x1_done)
        aff_mm = None
        for g in range(NK):
            for k in range(NK):
                aff_mm = mm(PS[:, 3072 + g * 16: 3072 + (g + 1) * 16],
                            BIG1[:, k * 2048 + g * 128: k * 2048 + (g + 1) * 128],
                            CTBb[:, k * 16:(k + 1) * 16],
                            k == 0, k == NK - 1,
                            inc="pe" if (g == NK - 1 and k == NK - 1) else None)
        P.wait("scalar", "pe", aff_mm)
        sig_all = P.op("scalar", lambda s: s.activation(AFFT[:, :], PS[:, 3072:3328], AF.Sigmoid), inc="act")
        P.wait("vector", "act", sig_all)

        def g3(apv):
            return apv.rearrange("p (g e) -> p g e", e=16)

        def bc(apv):
            return apv.unsqueeze(-1).broadcast_to([128, 16, 16])

        def dv(fn):
            c = P.op("vector", fn, inc="dve")
            P.wait("vector", "dve", c)
            return c
        dv(lambda v: v.tensor_tensor(g3(BTOK[:, :]), g3(AFFT[:, :]), GBBC[:, :].unsqueeze(1).broadcast_to([128, 16, 16]), ALU.add))
        dv(lambda v: v.reduce_max(M1R[:, :], g3(BTOK[:, :]), axis=AX.X))
        dv(lambda v: v.tensor_tensor(g3(T1R[:, :]), g3(BTOK[:, :]), bc(M1R[:, :]), ALU.is_ge))
        dv(lambda v: v.scalar_tensor_tensor(B2R[:, :], T1R[:, :], -1e9, BTOK[:, :], op0=ALU.mult, op1=ALU.add))
        dv(lambda v: v.reduce_max(M1R[:, :], g3(B2R[:, :]), axis=AX.X))
        dv(lambda v: v.tensor_tensor(g3(T2R[:, :]), g3(B2R[:, :]), bc(M1R[:, :]), ALU.is_ge))
        dv(lambda v: v.tensor_tensor(T1R[:, :], T1R[:, :], T2R[:, :], ALU.add))
        dv(lambda v: v.tensor_tensor(WSEL[:, :], AFFT[:, :], T1R[:, :], ALU.mult))
        dv(lambda v: v.reduce_sum(DENR[:, :], g3(WSEL[:, :]), axis=AX.X))
        dv(lambda v: v.tensor_scalar(DENR[:, :], DENR[:, :], MOE_EPS, None, ALU.add))
        dv(lambda v: v.reciprocal(DINV[:, :], DENR[:, :]))
        rt_all = dv(lambda v: v.tensor_tensor(g3(MSKT[:, :]), g3(WSEL[:, :]), bc(DINV[:, :]), ALU.mult))
        if debug:
            P.wait("gpsimd", "dve", rt_all)
            gdma(dbg_aff, AFFT[:, 0:16])
            gdma(dbg_btok, BTOK[:, 0:16])
            gdma(dbg_m1, M1R[:, 0:1])
            gdma(dbg_t1r, T1R[:, 0:16])
            dbg_rt_q = gdma(dbg_mskt, MSKT[:, 0:16])
        mcp_g = {}
        P.wait("tensor", "dve", rt_all)
        for g in range(NK):
            if g >= 1:
                P.wait("tensor", "act", mcp_g[g - 1])
            tp = P.op("tensor", lambda t, g=g: t.transpose(PS[0:16, 3584:3712], MSKT[:, g * 16:(g + 1) * 16], EYEf[:, :]), inc="pe")
            P.wait("scalar", "pe", tp)
            mcp_g[g] = P.op("scalar", lambda s, g=g: s.copy(MASKF[:, g * 128:(g + 1) * 128], PS[0:16, 3584:3712]), inc="act")
        mask_done = mcp_g[NK - 1]
        if debug:
            P.wait("gpsimd", "act", mask_done)
            gdma(dbg_mask, MASKF[:, :])

        passes = [("s", 0), ("s", 1)] + [("e", e) for e in range(NE - SKIP_EXPERTS)]

        def wsrc(kind, idx, which):
            if kind == "s":
                return {"g": dr["wgs_r"], "u": dr["wus_r"], "d": dr["wds_r"]}[which][idx]
            return {"g": dr["wg_r"], "u": dr["wu_r"], "d": dr["wd_r"]}[which][idx]

        load_q = {}
        cons_mark = {}
        nld = [0]

        def emit_load(src):
            n = nld[0]
            nld[0] += 1
            if n >= 2:
                sem, val = cons_mark[n - 2]
                P.wait("gpsimd", sem, val)
            else:
                P.wait("gpsimd", "dve", x1_done)
                P.wait("gpsimd", "pe", mixer_pe_done)
            load_q[n] = gdma(SLOT[n % 2][:, :], src)
            return n

        acc_last = {}
        prev_acc_even = None
        prev_acc_odd = None
        um7_prev = None
        silu7_prev = None
        prev_mb_ev = None
        ln_dve_last = None
        ln_store_last = None
        first_acc_written = False

        for hf in range(2):
            th = hf * HFT
            for j, (kind, idx) in enumerate(passes):
                n_g = emit_load(wsrc(kind, idx, "g"))
                gslot = SLOT[n_g % 2]
                n_u = emit_load(wsrc(kind, idx, "u"))
                uslot = SLOT[n_u % 2]

                if kind == "e":
                    if hf == 0 and j == 2:
                        P.wait("tensor", "act", mask_done)
                    if prev_mb_ev is not None:
                        P.wait("tensor", "act", prev_mb_ev)
                    if ln_dve_last is not None:
                        P.wait("tensor", "dve", ln_dve_last)
                        ln_dve_last = None
                    mb_mm = None
                    for tc in range(2):
                        mb_mm = mm(PS[:, 2048 + tc * 512: 2048 + (tc + 1) * 512],
                                   SELBsb[:, idx * 128:(idx + 1) * 128],
                                   MASKF[:, th + tc * 512: th + (tc + 1) * 512],
                                   True, True, inc="pe" if tc == 1 else None)
                    P.wait("scalar", "pe", mb_mm)
                    if um7_prev is not None:
                        P.wait("scalar", "dve", um7_prev)
                    mb_ev = P.op("scalar", lambda s: s.copy(MBSB[:, :], PS[:, 2048:3072]), inc="act")
                    prev_mb_ev = mb_ev
                else:
                    mb_ev = None

                silu_i = {}
                h_i = {}
                um_i = {}
                g_done_i = {}
                u_done_i = {}
                for i in range(8):
                    pb = (i % 2) * 1024
                    if i == 0:
                        P.wait("tensor", "qg", load_q[n_g])
                        if prev_acc_even is not None:
                            P.wait("tensor", "dve", prev_acc_even)
                    elif i == 1:
                        if prev_acc_odd is not None:
                            P.wait("tensor", "dve", prev_acc_odd)
                    else:
                        P.wait("tensor", "act", silu_i[i - 2])
                    for k in range(NK):
                        for tc in range(2):
                            g_done_i[i] = mm(PS[:, pb + tc * 512: pb + (tc + 1) * 512],
                                             gslot[:, k * 1024 + i * 128: k * 1024 + (i + 1) * 128],
                                             BIG1[:, k * 2048 + th + tc * 512: k * 2048 + th + (tc + 1) * 512],
                                             k == 0, k == NK - 1,
                                             inc="pe" if (k == NK - 1 and tc == 1) else None)
                    P.wait("scalar", "pe", g_done_i[i])
                    if j == 0 and ln_dve_last is not None:
                        if i == 4:
                            P.wait("scalar", "dve", ln_dve_last)
                        if i == 6:
                            P.wait("scalar", "qs", ln_store_last)
                    silu_i[i] = P.op("scalar",
                                     lambda s, i=i, pb=pb: s.activation(HBUF[:, i * 1024:(i + 1) * 1024], PS[:, pb:pb + 1024], AF.Silu),
                                     inc="act")
                cons_mark[n_g] = ("pe", g_done_i[7])
                silu7_prev = silu_i[7]
                for i in range(8):
                    pb = (i % 2) * 1024
                    if i == 0:
                        P.wait("tensor", "qg", load_q[n_u])
                        P.wait("tensor", "act", silu_i[6])
                    elif i == 1:
                        P.wait("tensor", "act", silu_i[7])
                    else:
                        P.wait("tensor", "dve", um_i[i - 2])
                    for k in range(NK):
                        for tc in range(2):
                            u_done_i[i] = mm(PS[:, pb + tc * 512: pb + (tc + 1) * 512],
                                             uslot[:, k * 1024 + i * 128: k * 1024 + (i + 1) * 128],
                                             BIG1[:, k * 2048 + th + tc * 512: k * 2048 + th + (tc + 1) * 512],
                                             k == 0, k == NK - 1,
                                             inc="pe" if (k == NK - 1 and tc == 1) else None)
                    P.wait("vector", "pe", u_done_i[i])
                    if kind == "e":
                        if i == 0:
                            P.wait("vector", "act", mb_ev)
                        um_i[i] = P.op("vector",
                                       lambda v, i=i, pb=pb: v.tensor_tensor(UM[i % 2][:, :], PS[:, pb:pb + 1024], MBSB[:, :], ALU.mult),
                                       inc="dve")
                    else:
                        um_i[i] = P.op("vector",
                                       lambda v, i=i, pb=pb: v.tensor_scalar(UM[i % 2][:, :], PS[:, pb:pb + 1024], RATIO, None, ALU.mult),
                                       inc="dve")
                    P.wait("vector", "act", silu_i[i])
                    if debug and not first_acc_written and i == 0:
                        P.wait("vector", "qg", q_dbg_ys)
                    h_i[i] = P.op("vector",
                                  lambda v, i=i: v.tensor_tensor(HBUF[:, i * 1024:(i + 1) * 1024], HBUF[:, i * 1024:(i + 1) * 1024], UM[i % 2][:, :], ALU.mult),
                                  inc="dve")
                cons_mark[n_u] = ("pe", u_done_i[7])
                um7_prev = um_i[7]

                n_d = emit_load(wsrc(kind, idx, "d"))
                dslot = SLOT[n_d % 2]

                wd_d = {}
                for d in range(NK):
                    if d == 0:
                        P.wait("tensor", "qg", load_q[n_d])
                        P.wait("tensor", "dve", h_i[6])
                    if d >= 2:
                        P.wait("tensor", "dve", acc_last[d - 2])
                    pb = (d % 2) * 1024
                    for k in range(8):
                        if d == 0 and k == 7:
                            P.wait("tensor", "dve", h_i[7])
                        for tc in range(2):
                            wd_d[d] = mm(PS[:, pb + tc * 512: pb + (tc + 1) * 512],
                                         dslot[:, k * 2048 + d * 128: k * 2048 + (d + 1) * 128],
                                         HBUF[:, k * 1024 + tc * 512: k * 1024 + (tc + 1) * 512],
                                         k == 0, k == 7,
                                         inc="pe" if (k == 7 and tc == 1) else None)
                    P.wait("vector", "pe", wd_d[d])
                    if j == 0:
                        acc_last[d] = P.op("vector",
                                           lambda v, d=d, pb=pb, th=th: v.tensor_tensor(
                                               ACC[:, d * 1024:(d + 1) * 1024],
                                               BIG1[:, d * 2048 + th: d * 2048 + th + 1024],
                                               PS[:, pb:pb + 1024], ALU.add),
                                           inc="dve")
                        first_acc_written = True
                    else:
                        acc_last[d] = P.op("vector",
                                           lambda v, d=d, pb=pb: v.tensor_tensor(
                                               ACC[:, d * 1024:(d + 1) * 1024],
                                               ACC[:, d * 1024:(d + 1) * 1024],
                                               PS[:, pb:pb + 1024], ALU.add),
                                           inc="dve")
                cons_mark[n_d] = ("pe", wd_d[NK - 1])
                prev_acc_even = acc_last[14]
                prev_acc_odd = acc_last[15]
                if debug and hf == 0 and j == len(passes) - 1:
                    P.wait("gpsimd", "dve", acc_last[NK - 1])
                    q_hard = gdma(dbg_hb, ACC[:, :])
                    P.wait("gpsimd", "qg", q_hard)

            zqf = {}
            fstat = {}
            for d in range(NK):
                P.wait("scalar", "dve", acc_last[NK - 1] if d == 0 else acc_last[d])
                if d >= 2:
                    P.wait("scalar", "pe", fstat[d - 2])
                zqf[d] = P.op("scalar",
                              lambda s, d=d: s.activation(ZSQF[d % 2][:, :], ACC[:, d * 1024:(d + 1) * 1024], AF.Square),
                              inc="act")
                if d == 0:
                    P.wait("tensor", "dve", acc_last[NK - 1])
                    if prev_mb_ev is not None:
                        P.wait("tensor", "act", prev_mb_ev)
                P.wait("tensor", "act", zqf[d])
                for tc in range(2):
                    mm(PS[0:1, 2048 + tc * 512: 2048 + (tc + 1) * 512], ONESCOLb[:, :],
                       ACC[:, d * 1024 + tc * 512: d * 1024 + (tc + 1) * 512],
                       d == 0, d == NK - 1)
                    fstat[d] = mm(PS[0:1, 3072 + tc * 512: 3072 + (tc + 1) * 512], ONESCOLb[:, :],
                                  ZSQF[d % 2][:, tc * 512:(tc + 1) * 512],
                                  d == 0, d == NK - 1,
                                  inc="pe" if tc == 1 else None)
            P.wait("vector", "pe", fstat[NK - 1])
            def dvf(fn):
                c = P.op("vector", fn, inc="dve")
                P.wait("vector", "dve", c)
                return c
            dvf(lambda v: v.tensor_scalar(FR1[:, :], PS[0:1, 2048:3072], 1.0 / D, None, ALU.mult))
            dvf(lambda v: v.tensor_scalar(FR2[:, :], PS[0:1, 3072:4096], 1.0 / D, None, ALU.mult))
            dvf(lambda v: v.tensor_tensor(PS[0:1, 2048:3072], FR1[:, :], FR1[:, :], ALU.mult))
            dvf(lambda v: v.tensor_tensor(PS[0:1, 3072:4096], FR2[:, :], PS[0:1, 2048:3072], ALU.subtract))
            f_var = dvf(lambda v: v.tensor_scalar(PS[0:1, 3072:4096], PS[0:1, 3072:4096], LN_EPS, None, ALU.add))
            P.wait("scalar", "dve", f_var)
            f_ln = P.op("scalar", lambda s: s.activation(PS[0:1, 2048:3072], PS[0:1, 3072:4096], AF.Ln), inc="act")
            P.wait("vector", "act", f_ln)
            f_lh = dvf(lambda v: v.tensor_scalar(PS[0:1, 2048:3072], PS[0:1, 2048:3072], -0.5, None, ALU.mult))
            P.wait("scalar", "dve", f_lh)
            f_rstd = P.op("scalar", lambda s: s.activation(FR2[:, :], PS[0:1, 2048:3072], AF.Exp), inc="act")
            P.wait("vector", "act", f_rstd)
            f_mr = dvf(lambda v: v.tensor_tensor(FR1[:, :], FR1[:, :], FR2[:, :], ALU.mult))
            P.wait("tensor", "dve", f_mr)
            fbc = None
            for tc in range(2):
                mm(PS[:, 2048 + tc * 512: 2048 + (tc + 1) * 512], ONES1b[:, :],
                   FR2[0:1, tc * 512:(tc + 1) * 512], True, True)
                fbc = mm(PS[:, 3072 + tc * 512: 3072 + (tc + 1) * 512], ONES1b[:, :],
                         FR1[0:1, tc * 512:(tc + 1) * 512], True, True,
                         inc="pe" if tc == 1 else None)
            st_q = {}
            out_d = {}
            P.wait("vector", "pe", fbc)
            for d in range(NK):
                if d >= 2:
                    P.wait("vector", "qs", st_q[d - 2])
                P.op("vector", lambda v, d=d: v.tensor_tensor(
                    T1F[:, :], ACC[:, d * 1024:(d + 1) * 1024], PS[:, 2048:3072], ALU.mult),
                    inc="dve")
                out_d[d] = P.op("vector", lambda v, d=d: v.tensor_tensor(
                    OST[d % 2][:, :], T1F[:, :], PS[:, 3072:4096], ALU.subtract),
                    inc="dve")
                P.wait("sync", "dve", out_d[d])
                st_q[d] = sdma(outT[d * 128:(d + 1) * 128, th:th + HFT], OST[d % 2][:, :])
            P.wait("vector", "qs", st_q[NK - 1])
            ln_dve_last = out_d[NK - 1]
            ln_store_last = st_q[NK - 1]

        P.wait("sync", "qs", P.cnt["qs"])
        if debug:
            P.wait("gpsimd", "qg", P.cnt["qg"])

        P.check()

        with nc.Block() as block:
            block.tensor(lambda t: P.replay(t, "tensor"))
            block.vector(lambda v: P.replay(v, "vector"))
            block.scalar(lambda s: P.replay(s, "scalar"))
            block.gpsimd(lambda g: P.replay(g, "gpsimd"))
            block.sync(lambda s: P.replay(s, "sync"))

    return nc


def _rearr(a, kp=128):
    K = a.shape[0] // kp
    return np.ascontiguousarray(a.reshape(K, kp, a.shape[1]).transpose(1, 0, 2).reshape(kp, K * a.shape[1]))


def kernel(L0_attn_x, L0_attn_W_in, L0_attn_W_out, L0_attn_log_A, L0_attn_W_dt,
           L0_attn_b_dt, L0_ffn_centroids, L0_ffn_gate_bias, L0_ffn_wg_s,
           L0_ffn_wu_s, L0_ffn_wd_s, L0_ffn_wg, L0_ffn_wu, L0_ffn_wd):
    f = np.float32
    x = np.asarray(L0_attn_x, f)
    W_in = np.asarray(L0_attn_W_in, f)
    W_out = np.asarray(L0_attn_W_out, f)
    log_A = np.asarray(L0_attn_log_A, f)
    W_dt = np.asarray(L0_attn_W_dt, f)
    b_dt = np.asarray(L0_attn_b_dt, f)
    cent = np.asarray(L0_ffn_centroids, f)
    gb = np.asarray(L0_ffn_gate_bias, f)
    wg_s = np.asarray(L0_ffn_wg_s, f)
    wu_s = np.asarray(L0_ffn_wu_s, f)
    wd_s = np.asarray(L0_ffn_wd_s, f)
    wg = np.asarray(L0_ffn_wg, f)
    wu = np.asarray(L0_ffn_wu, f)
    wd = np.asarray(L0_ffn_wd, f)

    winT_r = _rearr(np.ascontiguousarray(W_in.T))
    woutT = W_out.T
    woutT_q = np.stack([_rearr(np.ascontiguousarray(woutT[:, q * 512:(q + 1) * 512]))
                        for q in range(4)])
    wdtT_r = _rearr(np.ascontiguousarray(W_dt.T))
    centT_r = _rearr(np.ascontiguousarray(cent.T))
    logA_r = np.ascontiguousarray(log_A.reshape(-1).reshape(16, 128).T)
    bdt_c = np.ascontiguousarray(b_dt.reshape(32, 1))
    gb_row = np.ascontiguousarray(gb.reshape(1, 16))
    wg_r = np.stack([_rearr(wg[e]) for e in range(NE)])
    wu_r = np.stack([_rearr(wu[e]) for e in range(NE)])
    wd_r = np.stack([_rearr(wd[e]) for e in range(NE)])
    wgs_r = np.stack([_rearr(np.ascontiguousarray(wg_s[:, s * 1024:(s + 1) * 1024])) for s in range(2)])
    wus_r = np.stack([_rearr(np.ascontiguousarray(wu_s[:, s * 1024:(s + 1) * 1024])) for s in range(2)])
    wds_r = np.stack([_rearr(np.ascontiguousarray(wd_s[s * 1024:(s + 1) * 1024, :])) for s in range(2)])
    SEL = np.kron(np.eye(32, dtype=f), np.ones((1, 64), f))
    SELB = np.kron(np.eye(16, dtype=f), np.ones((1, 128), f))
    EYE = np.eye(128, dtype=f)
    ONESCOL = np.ones((128, 1), f)
    ONES1 = np.ones((1, 128), f)

    shared = {
        "winT_r": winT_r, "woutT_q": woutT_q, "wdtT_r": wdtT_r, "centT_r": centT_r,
        "logA_r": logA_r, "bdt_c": bdt_c, "gb_row": gb_row,
        "wg_r": wg_r, "wu_r": wu_r, "wd_r": wd_r,
        "wgs_r": wgs_r, "wus_r": wus_r, "wds_r": wds_r,
        "SEL": SEL, "SELB": SELB, "EYE": EYE, "ONESCOL": ONESCOL, "ONES1": ONES1,
    }
    in_maps = []
    for c in range(8):
        m = dict(shared)
        m["xT_r"] = _rearr(np.ascontiguousarray(x[c].T))
        in_maps.append(m)

    nc = build_program(debug=DEBUG)
    res = run_bass_kernel_spmd(nc, in_maps, core_ids=list(range(8)))
    out = np.stack([np.ascontiguousarray(r["outT"].T) for r in res.results])
    if DEBUG:
        kernel._debug = res.results
    return out.astype(np.float32)


# version 08 (final)
# speedup vs baseline: 1.0921x; 1.0921x over previous
import numpy as np
import concourse.bass as bass
import concourse.mybir as mybir
from concourse.bass_utils import run_bass_kernel_spmd

F32 = mybir.dt.float32
BF16 = mybir.dt.bfloat16
AF = mybir.ActivationFunctionType
ALU = mybir.AluOpType
AX = mybir.AxisListType

D = 2048
T = 2048
NK = 16
NE = 16
HFT = 1024
LN_EPS = 1e-5
MOE_EPS = 1e-9
RATIO = 0.1

DEBUG = False
SKIP_EXPERTS = 0

ENGS = ("tensor", "vector", "scalar", "gpsimd", "sync")


class Prog:
    def __init__(self):
        self.ops = {e: [] for e in ENGS}
        self.sems = {}
        self.cnt = {}

    def add_sem(self, name, handle):
        self.sems[name] = handle
        self.cnt[name] = 0

    def op(self, eng, fn, inc=None, amt=1):
        if inc is None:
            self.ops[eng].append((fn, None, 0, ("op", None, 0)))
            return None
        self.cnt[inc] += amt
        self.ops[eng].append((fn, self.sems[inc], amt, ("inc", inc, amt)))
        return self.cnt[inc]

    def wait(self, eng, sem, val):
        if val is None or val <= 0:
            return
        h = self.sems[sem]
        self.ops[eng].append((lambda e, h=h, v=val: e.wait_ge(h, v), None, 0,
                              ("wait", sem, val)))

    def replay(self, eng_obj, eng_name):
        for fn, semh, amt, _meta in self.ops[eng_name]:
            instr = fn(eng_obj)
            if semh is not None:
                instr.then_inc(semh, amt)

    def check(self):
        counts = {s: 0 for s in self.cnt}
        ptrs = {e: 0 for e in self.ops}
        total = sum(len(v) for v in self.ops.values())
        done = 0
        while done < total:
            progress = False
            for e in self.ops:
                lst = self.ops[e]
                while ptrs[e] < len(lst):
                    _fn, _h, _amt, meta = lst[ptrs[e]]
                    if meta[0] == "wait" and counts[meta[1]] < meta[2]:
                        break
                    if meta[0] == "inc":
                        counts[meta[1]] += meta[2]
                    ptrs[e] += 1
                    done += 1
                    progress = True
            if not progress:
                blocked = {e: (ptrs[e], self.ops[e][ptrs[e]][3] if ptrs[e] < len(self.ops[e]) else None)
                           for e in self.ops if ptrs[e] < len(self.ops[e])}
                raise RuntimeError(f"DEADLOCK: counts={counts} blocked={blocked}")
        return True


def build_program(debug=False):
    nc = bass.Bass("TRN2", target_bir_lowering=False)

    dr = {}

    def din(name, shape, dt=F32):
        dr[name] = nc.dram_tensor(name, shape, dt, kind="ExternalInput").ap()

    din("xT_r", [128, NK * T])
    din("winT_r", [128, NK * D])
    din("woutT_q", [4, 128, NK * 512])
    din("wdtT_r", [128, NK * 32])
    din("centT_r", [128, NK * 16])
    din("logA_r", [128, 16])
    din("bdt_c", [32, 1])
    din("gb_row", [1, 16])
    din("wg_r", [NE, 128, NK * 1024])
    din("wu_r", [NE, 128, NK * 1024])
    din("wd_r", [NE, 128, 8 * D])
    din("wgs_r", [2, 128, NK * 1024])
    din("wus_r", [2, 128, NK * 1024])
    din("wds_r", [2, 128, 8 * D])
    din("SEL", [32, D])
    din("SELB", [16, D])
    din("EYE", [128, 128])
    din("ONESCOL", [128, 1])
    din("ONES1", [1, 128])

    outT = nc.dram_tensor("outT", [D, T], F32, kind="ExternalOutput").ap()
    if debug:
        dbg_xp = nc.dram_tensor("dbg_xp", [128, NK * T], F32, kind="ExternalOutput").ap()
        dbg_ys = nc.dram_tensor("dbg_ys", [128, NK * T], F32, kind="ExternalOutput").ap()
        dbg_x1 = nc.dram_tensor("dbg_x1", [128, NK * T], F32, kind="ExternalOutput").ap()
        dbg_mask = nc.dram_tensor("dbg_mask", [16, T], F32, kind="ExternalOutput").ap()
        dbg_dt = nc.dram_tensor("dbg_dt", [32, T], F32, kind="ExternalOutput").ap()
        dbg_aff = nc.dram_tensor("dbg_aff", [128, 16], F32, kind="ExternalOutput").ap()
        dbg_btok = nc.dram_tensor("dbg_btok", [128, 16], F32, kind="ExternalOutput").ap()
        dbg_m1 = nc.dram_tensor("dbg_m1", [128, 1], F32, kind="ExternalOutput").ap()
        dbg_t1r = nc.dram_tensor("dbg_t1r", [128, 16], F32, kind="ExternalOutput").ap()
        dbg_mskt = nc.dram_tensor("dbg_mskt", [128, 16], F32, kind="ExternalOutput").ap()
        dbg_acc = nc.dram_tensor("dbg_acc", [128, 16384], F32, kind="ExternalOutput").ap()
        dbg_hb = nc.dram_tensor("dbg_hb", [128, 16384], F32, kind="ExternalOutput").ap()

    SB0 = ((nc.sbuf_base + 63) // 64) * 64

    def at(name, shape, dt, off):
        return nc.alloc_sbuf_tensor_at(name, shape, dt, offset=SB0 + off).ap()

    KB = 1024
    BIG1 = at("BIG1", [128, 32768], BF16, 0)
    BIG2 = at("BIG2", [128, 32768], BF16, 64 * KB)
    A0 = 128 * KB
    WIB = at("WIB", [128, 32768], BF16, A0)
    SELsb = at("SELsb", [32, 2048], BF16, A0)
    DTS = at("DTS", [32, 2048], BF16, A0 + 4 * KB)
    WDTb = at("WDTb", [128, 512], BF16, A0 + 8 * KB)
    XRB = at("XRB", [128, 2048], BF16, A0 + 12 * KB)
    ASB = [at("ASB0", [128, 2048], F32, A0 + 12 * KB),
           at("ASB1", [128, 2048], F32, A0 + 20 * KB)]
    ZSQ = [at("ZSQ0", [128, 2048], BF16, A0 + 16 * KB),
           at("ZSQ1", [128, 2048], BF16, A0 + 20 * KB)]
    T1 = at("T1", [128, 2048], BF16, A0 + 28 * KB)
    RBSB = at("RBSB", [128, 2048], BF16, A0 + 16 * KB)
    MRSB = at("MRSB", [128, 2048], BF16, A0 + 20 * KB)
    R1 = at("R1", [1, 2048], BF16, A0 + 24 * KB)
    R2 = at("R2", [1, 2048], BF16, A0 + 28 * KB)
    WOBQ = [at("WOBQ0", [128, 8192], BF16, A0 + 32 * KB),
            at("WOBQ1", [128, 8192], BF16, A0 + 48 * KB)]
    SLOT = [at("SLOTA", [128, 16384], BF16, A0),
            at("SLOTB", [128, 16384], BF16, A0 + 32 * KB)]
    B2 = 64 * KB
    ACC = at("ACC", [128, 16384], BF16, B2)
    HBUF = at("HBUF", [128, 8192], BF16, B2 + 32 * KB)
    AFFT = at("AFFT", [128, 256], F32, B2 + 48 * KB)
    BTOK = at("BTOK", [128, 256], F32, B2 + 49 * KB)
    B2R = at("B2R", [128, 256], F32, B2 + 50 * KB)
    T1R = at("T1R", [128, 256], F32, B2 + 51 * KB)
    T2R = at("T2R", [128, 256], F32, B2 + 62 * KB)
    WSEL = at("WSEL", [128, 256], F32, B2 + 63 * KB)
    XSTG = at("XSTG", [128, 16384], F32, 0)
    UM = [at("UM0", [128, 1024], BF16, B2 + 52 * KB),
          at("UM1", [128, 1024], BF16, B2 + 54 * KB)]
    MBSB = at("MBSB", [128, 1024], BF16, B2 + 56 * KB)
    OST = [at("OST0", [128, 1024], F32, B2 + 58 * KB),
           at("OST1", [128, 1024], F32, B2 + 44 * KB)]
    ZSQF = [at("ZSQF0", [128, 1024], BF16, B2 + 32 * KB),
            at("ZSQF1", [128, 1024], BF16, B2 + 34 * KB)]
    FR1 = at("FR1", [1, 1024], BF16, B2 + 36 * KB)
    FR2 = at("FR2", [1, 1024], BF16, B2 + 38 * KB)
    T1F = at("T1F", [128, 1024], F32, B2 + 40 * KB)

    S0 = 192 * KB
    SELBsb = at("SELBsb", [16, 2048], BF16, S0)
    MASKF = at("MASKF", [16, 2048], BF16, S0 + 4 * KB)
    CTBb = at("CTBb", [128, 256], BF16, S0 + 8 * KB)
    EYEf = at("EYEf", [128, 128], F32, S0 + 8 * KB + 512)
    ONESCOLb = at("ONESCOLb", [128, 1], BF16, S0 + 9 * KB)
    ONES1b = at("ONES1b", [1, 128], BF16, S0 + 9 * KB + 64)
    ONES1f = at("ONES1f", [1, 128], F32, S0 + 9 * KB + 384)
    LOGR = at("LOGR", [128, 16], F32, S0 + 10 * KB)
    ACsb = at("ACsb", [128, 16], F32, S0 + 10 * KB + 64)
    BDT = at("BDT", [32, 1], F32, S0 + 10 * KB + 128)
    GBROW = at("GBROW", [1, 16], F32, S0 + 10 * KB + 192)
    GBBC = at("GBBC", [128, 16], F32, S0 + 10 * KB + 256)
    MSKT = at("MSKT", [128, 256], F32, S0 + 10 * KB + 320)
    M1R = at("M1R", [128, 16], F32, S0 + 11 * KB + 384)
    DENR = at("DENR", [128, 16], F32, S0 + 11 * KB + 448)
    DINV = at("DINV", [128, 16], F32, S0 + 11 * KB + 512)

    P = Prog()

    import contextlib
    stack = contextlib.ExitStack()
    with stack:
        PS = stack.enter_context(nc.psum_tensor("PS", [128, 4096], F32))
        for s in ("qg", "qs", "pe", "act", "dve"):
            P.add_sem(s, stack.enter_context(nc.semaphore(s)))

        def gdma(dst, src):
            return P.op("gpsimd", lambda g, d=dst, s=src: g.dma_start(out=d, in_=s),
                        inc="qg", amt=16)

        def sdma(dst, src):
            return P.op("sync", lambda g, d=dst, s=src: g.dma_start(out=d, in_=s),
                        inc="qs", amt=16)

        def mm(out, lhsT, rhs, start, stop, inc=None):
            return P.op("tensor",
                        lambda t, o=out, l=lhsT, r=rhs, st=start, sp=stop:
                        t.matmul(o, l, r, start=st, stop=sp),
                        inc=inc)

        q_xb = gdma(BIG2[:, 0:16384], dr["xT_r"][:, 0:16384])
        q_win = []
        for e in range(NK):
            q_win.append(gdma(WIB[:, e * 2048:(e + 1) * 2048],
                              dr["winT_r"][:, e * 2048:(e + 1) * 2048]))
        q_const = []
        q_const.append(gdma(SELBsb[:, :], dr["SELB"]))
        q_const.append(gdma(CTBb[:, :], dr["centT_r"]))
        q_const.append(gdma(EYEf[:, :], dr["EYE"]))
        q_const.append(gdma(ONESCOLb[:, :], dr["ONESCOL"]))
        q_const.append(gdma(ONES1b[:, :], dr["ONES1"]))
        q_const.append(gdma(ONES1f[:, :], dr["ONES1"]))
        q_const.append(gdma(LOGR[:, :], dr["logA_r"]))
        q_const.append(gdma(BDT[:, :], dr["bdt_c"]))
        q_const.append(gdma(GBROW[:, :], dr["gb_row"]))
        q_x2 = sdma(XSTG[:, :], dr["xT_r"][:, 16384:32768])
        P.wait("scalar", "qs", q_x2)
        c_x2 = P.op("scalar", lambda s: s.copy(BIG2[:, 16384:32768], XSTG[:, :]), inc="act")

        P.wait("scalar", "qg", q_const[8])
        c_exp = P.op("scalar", lambda s: s.activation(ACsb[:, :], LOGR[:, :], AF.Exp),
                     inc="act")
        P.wait("vector", "act", c_exp)
        c_neg = P.op("vector",
                     lambda v: v.tensor_scalar(ACsb[:, :], ACsb[:, :], -1.0, None, ALU.mult),
                     inc="dve")
        P.wait("tensor", "qg", q_const[8])
        c_gb = mm(PS[:, 3584:3600], ONES1f[:, :], GBROW[:, :], True, True, inc="pe")
        P.wait("vector", "pe", c_gb)
        P.op("vector", lambda v: v.tensor_copy(GBBC[:, :], PS[:, 3584:3600]), inc="dve")

        xp_ev = {}
        P.wait("tensor", "qg", q_xb)
        P.wait("tensor", "act", c_x2)
        for e in range(NK):
            pb = (e % 2) * 2048
            P.wait("tensor", "qg", q_win[e])
            if e >= 2:
                P.wait("tensor", "act", xp_ev[e - 2])
            xp_mm = None
            for k in range(NK):
                for tc in range(4):
                    xp_mm = mm(PS[:, pb + tc * 512: pb + (tc + 1) * 512],
                               WIB[:, e * 2048 + k * 128: e * 2048 + (k + 1) * 128],
                               BIG2[:, k * 2048 + tc * 512: k * 2048 + (tc + 1) * 512],
                               k == 0, k == NK - 1,
                               inc="pe" if (k == NK - 1 and tc == 3) else None)
            P.wait("scalar", "pe", xp_mm)
            xp_ev[e] = P.op("scalar",
                            lambda s, e=e, pb=pb: s.copy(BIG1[:, e * 2048:(e + 1) * 2048],
                                                         PS[:, pb:pb + 2048]),
                            inc="act")
        xp_done_pe = P.cnt["pe"]
        xp_ev_all = xp_ev[NK - 1]

        if debug:
            P.wait("gpsimd", "act", xp_ev_all)
            q_dbg_xp = gdma(dbg_xp, BIG1[:, :])

        P.wait("gpsimd", "pe", xp_done_pe)
        q_sel = gdma(SELsb[:, :], dr["SEL"])
        q_wdt = gdma(WDTb[:, :], dr["wdtT_r"])
        q_wob = [gdma(WOBQ[0][:, :], dr["woutT_q"][0]),
                 gdma(WOBQ[1][:, :], dr["woutT_q"][1])]

        P.wait("tensor", "qg", q_wdt)
        P.wait("tensor", "act", xp_ev_all)
        dt_mm = None
        for k in range(NK):
            for tc in range(4):
                dt_mm = mm(PS[0:32, tc * 512:(tc + 1) * 512],
                           WDTb[:, k * 32:(k + 1) * 32],
                           BIG1[:, k * 2048 + tc * 512: k * 2048 + (tc + 1) * 512],
                           k == 0, k == NK - 1,
                           inc="pe" if (k == NK - 1 and tc == 3) else None)
        P.wait("scalar", "pe", dt_mm)
        P.op("scalar", lambda s: s.activation(PS[0:32, 2048:4096], PS[0:32, 0:2048],
                                              AF.Exp, bias=BDT[:, 0:1]), inc="act")
        dts_done = P.op("scalar", lambda s: s.activation(DTS[:, :], PS[0:32, 2048:4096],
                                                         AF.Ln, bias=1.0), inc="act")
        if debug:
            P.wait("gpsimd", "act", dts_done)
            gdma(dbg_dt, DTS[:, :])

        scan_done = {}
        aexp = {}
        P.wait("tensor", "act", dts_done)
        P.wait("tensor", "qg", q_sel)
        P.wait("scalar", "dve", c_neg)
        for ct in range(NK):
            if ct >= 1:
                P.wait("tensor", "act", aexp[ct - 1])
            sel_mm = None
            for tc in range(4):
                sel_mm = mm(PS[:, 2048 + tc * 512: 2048 + (tc + 1) * 512],
                            SELsb[:, ct * 128:(ct + 1) * 128],
                            DTS[:, tc * 512:(tc + 1) * 512],
                            True, True, inc="pe" if tc == 3 else None)
            P.wait("scalar", "pe", sel_mm)
            if ct >= 2:
                P.wait("scalar", "dve", scan_done[ct - 2])
            aexp[ct] = P.op("scalar",
                            lambda s, ct=ct: s.activation(ASB[ct % 2][:, :], PS[:, 2048:4096],
                                                          AF.Exp, scale=ACsb[:, ct:ct + 1]),
                            inc="act")
            P.wait("vector", "act", aexp[ct])
            scan_done[ct] = P.op("vector",
                                 lambda v, ct=ct: v.tensor_tensor_scan(
                                     BIG2[:, ct * 2048:(ct + 1) * 2048],
                                     ASB[ct % 2][:, :],
                                     BIG1[:, ct * 2048:(ct + 1) * 2048],
                                     0.0, ALU.mult, ALU.add),
                                 inc="dve")
            if ct == 0:
                P.wait("tensor", "act", dts_done)
                P.wait("tensor", "qg", q_wob[0])
            for kc in ([ct - 1] if ct >= 1 else []) + ([NK - 1] if ct == NK - 1 else []):
                P.wait("tensor", "dve", scan_done[kc])
                for tc in range(4):
                    y0_mm = mm(PS[:, tc * 512:(tc + 1) * 512],
                               WOBQ[0][:, kc * 512: kc * 512 + 128],
                               BIG2[:, kc * 2048 + tc * 512: kc * 2048 + (tc + 1) * 512],
                               kc == 0, kc == NK - 1,
                               inc="pe" if (kc == NK - 1 and tc == 3) else None)
        if debug:
            P.wait("gpsimd", "dve", scan_done[NK - 1])
            q_dbg_ys = gdma(dbg_ys, BIG2[:, :])

        z_done = {}
        y_done = {}
        y_done[0] = y0_mm
        wobq_next = 2
        for e in range(NK):
            if e >= 1:
                P.wait("gpsimd", "dve", z_done[e - 1])
            else:
                P.wait("gpsimd", "dve", scan_done[NK - 1])
            xrb_q = gdma(XRB[:, :], dr["xT_r"][:, e * 2048:(e + 1) * 2048])
            q = e // 4
            pb = (e % 2) * 2048
            if e >= 1:
                P.wait("tensor", "qg", q_wob[q])
            if e == 1:
                P.wait("tensor", "act", aexp[NK - 1])
            if e >= 2:
                P.wait("tensor", "dve", z_done[e - 2])
            for k in range(NK):
                for tc in range(4):
                    if e == 0:
                        break
                    y_done[e] = mm(PS[:, pb + tc * 512: pb + (tc + 1) * 512],
                                   WOBQ[q % 2][:, k * 512 + (e % 4) * 128: k * 512 + (e % 4 + 1) * 128],
                                   BIG2[:, k * 2048 + tc * 512: k * 2048 + (tc + 1) * 512],
                                   k == 0, k == NK - 1,
                                   inc="pe" if (k == NK - 1 and tc == 3) else None)
            if e % 4 == 3 and wobq_next < 4:
                P.wait("gpsimd", "pe", y_done[e])
                q_wob.append(gdma(WOBQ[wobq_next % 2][:, :], dr["woutT_q"][wobq_next]))
                wobq_next += 1
            P.wait("vector", "pe", y_done[e])
            P.wait("vector", "qg", xrb_q)
            if debug and e == 0:
                P.wait("vector", "qg", q_dbg_xp)
            z_done[e] = P.op("vector",
                             lambda v, e=e, pb=pb: v.tensor_tensor(
                                 BIG1[:, e * 2048:(e + 1) * 2048],
                                 XRB[:, :], PS[:, pb:pb + 2048], ALU.add),
                             inc="dve")

        zq = {}
        stat_mm = {}
        for e in range(NK):
            P.wait("scalar", "dve", z_done[e])
            if e >= 2:
                P.wait("scalar", "pe", stat_mm[e - 2])
            zq[e] = P.op("scalar",
                         lambda s, e=e: s.activation(ZSQ[e % 2][:, :],
                                                     BIG1[:, e * 2048:(e + 1) * 2048],
                                                     AF.Square),
                         inc="act")
            if e == 0:
                P.wait("tensor", "dve", z_done[NK - 1])
                P.wait("tensor", "act", aexp[NK - 1])
            P.wait("tensor", "act", zq[e])
            for tc in range(4):
                mm(PS[0:1, tc * 512:(tc + 1) * 512], ONESCOLb[:, :],
                   BIG1[:, e * 2048 + tc * 512: e * 2048 + (tc + 1) * 512],
                   e == 0, e == NK - 1)
                stat_mm[e] = mm(PS[0:1, 2048 + tc * 512: 2048 + (tc + 1) * 512], ONESCOLb[:, :],
                                ZSQ[e % 2][:, tc * 512:(tc + 1) * 512],
                                e == 0, e == NK - 1,
                                inc="pe" if tc == 3 else None)
        P.wait("vector", "pe", stat_mm[NK - 1])
        def dvr(fn):
            c = P.op("vector", fn, inc="dve")
            P.wait("vector", "dve", c)
            return c
        dvr(lambda v: v.tensor_scalar(R1[:, :], PS[0:1, 0:2048], 1.0 / D, None, ALU.mult))
        dvr(lambda v: v.tensor_scalar(R2[:, :], PS[0:1, 2048:4096], 1.0 / D, None, ALU.mult))
        dvr(lambda v: v.tensor_tensor(PS[0:1, 0:2048], R1[:, :], R1[:, :], ALU.mult))
        dvr(lambda v: v.tensor_tensor(PS[0:1, 2048:4096], R2[:, :], PS[0:1, 0:2048], ALU.subtract))
        r_var = dvr(lambda v: v.tensor_scalar(PS[0:1, 2048:4096], PS[0:1, 2048:4096], LN_EPS, None, ALU.add))
        P.wait("scalar", "dve", r_var)
        r_ln = P.op("scalar", lambda s: s.activation(PS[0:1, 0:2048], PS[0:1, 2048:4096], AF.Ln), inc="act")
        P.wait("vector", "act", r_ln)
        r_lh = dvr(lambda v: v.tensor_scalar(PS[0:1, 0:2048], PS[0:1, 0:2048], -0.5, None, ALU.mult))
        P.wait("scalar", "dve", r_lh)
        r_rstd = P.op("scalar", lambda s: s.activation(R2[:, :], PS[0:1, 0:2048], AF.Exp), inc="act")
        P.wait("vector", "act", r_rstd)
        r_mr = dvr(lambda v: v.tensor_tensor(R1[:, :], R1[:, :], R2[:, :], ALU.mult))
        P.wait("tensor", "dve", r_mr)
        bc_mm = None
        for tc in range(4):
            mm(PS[:, tc * 512:(tc + 1) * 512], ONES1b[:, :],
               R2[0:1, tc * 512:(tc + 1) * 512], True, True)
            bc_mm = mm(PS[:, 2048 + tc * 512: 2048 + (tc + 1) * 512], ONES1b[:, :],
                       R1[0:1, tc * 512:(tc + 1) * 512], True, True,
                       inc="pe" if tc == 3 else None)
        mixer_pe_done = bc_mm
        P.wait("scalar", "pe", bc_mm)
        P.op("scalar", lambda s: s.copy(RBSB[:, :], PS[:, 0:2048]), inc="act")
        bc_ev = P.op("scalar", lambda s: s.copy(MRSB[:, :], PS[:, 2048:4096]), inc="act")
        ap_done = {}
        P.wait("vector", "act", bc_ev)
        for e in range(NK):
            P.op("vector", lambda v, e=e: v.tensor_tensor(
                T1[:, :], BIG1[:, e * 2048:(e + 1) * 2048], RBSB[:, :], ALU.mult),
                inc="dve")
            ap_done[e] = P.op("vector", lambda v, e=e: v.tensor_tensor(
                BIG1[:, e * 2048:(e + 1) * 2048], T1[:, :], MRSB[:, :], ALU.subtract),
                inc="dve")
        x1_done = ap_done[NK - 1]
        if debug:
            P.wait("gpsimd", "dve", x1_done)
            gdma(dbg_x1, BIG1[:, :])

        P.wait("tensor", "dve", x1_done)
        aff_mm = None
        for g in range(NK):
            for k in range(NK):
                aff_mm = mm(PS[:, 3072 + g * 16: 3072 + (g + 1) * 16],
                            BIG1[:, k * 2048 + g * 128: k * 2048 + (g + 1) * 128],
                            CTBb[:, k * 16:(k + 1) * 16],
                            k == 0, k == NK - 1,
                            inc="pe" if (g == NK - 1 and k == NK - 1) else None)
        P.wait("scalar", "pe", aff_mm)
        sig_all = P.op("scalar", lambda s: s.activation(AFFT[:, :], PS[:, 3072:3328], AF.Sigmoid), inc="act")
        P.wait("vector", "act", sig_all)

        def g3(apv):
            return apv.rearrange("p (g e) -> p g e", e=16)

        def bc(apv):
            return apv.unsqueeze(-1).broadcast_to([128, 16, 16])

        def dv(fn):
            c = P.op("vector", fn, inc="dve")
            P.wait("vector", "dve", c)
            return c
        dv(lambda v: v.tensor_tensor(g3(BTOK[:, :]), g3(AFFT[:, :]), GBBC[:, :].unsqueeze(1).broadcast_to([128, 16, 16]), ALU.add))
        dv(lambda v: v.reduce_max(M1R[:, :], g3(BTOK[:, :]), axis=AX.X))
        dv(lambda v: v.tensor_tensor(g3(T1R[:, :]), g3(BTOK[:, :]), bc(M1R[:, :]), ALU.is_ge))
        dv(lambda v: v.scalar_tensor_tensor(B2R[:, :], T1R[:, :], -1e9, BTOK[:, :], op0=ALU.mult, op1=ALU.add))
        dv(lambda v: v.reduce_max(M1R[:, :], g3(B2R[:, :]), axis=AX.X))
        dv(lambda v: v.tensor_tensor(g3(T2R[:, :]), g3(B2R[:, :]), bc(M1R[:, :]), ALU.is_ge))
        dv(lambda v: v.tensor_tensor(T1R[:, :], T1R[:, :], T2R[:, :], ALU.add))
        dv(lambda v: v.tensor_tensor(WSEL[:, :], AFFT[:, :], T1R[:, :], ALU.mult))
        dv(lambda v: v.reduce_sum(DENR[:, :], g3(WSEL[:, :]), axis=AX.X))
        dv(lambda v: v.tensor_scalar(DENR[:, :], DENR[:, :], MOE_EPS, None, ALU.add))
        dv(lambda v: v.reciprocal(DINV[:, :], DENR[:, :]))
        rt_all = dv(lambda v: v.tensor_tensor(g3(MSKT[:, :]), g3(WSEL[:, :]), bc(DINV[:, :]), ALU.mult))
        if debug:
            P.wait("gpsimd", "dve", rt_all)
            gdma(dbg_aff, AFFT[:, 0:16])
            gdma(dbg_btok, BTOK[:, 0:16])
            gdma(dbg_m1, M1R[:, 0:1])
            gdma(dbg_t1r, T1R[:, 0:16])
            dbg_rt_q = gdma(dbg_mskt, MSKT[:, 0:16])
        mcp_g = {}
        P.wait("tensor", "dve", rt_all)
        for g in range(NK):
            if g >= 1:
                P.wait("tensor", "act", mcp_g[g - 1])
            tp = P.op("tensor", lambda t, g=g: t.transpose(PS[0:16, 3584:3712], MSKT[:, g * 16:(g + 1) * 16], EYEf[:, :]), inc="pe")
            P.wait("scalar", "pe", tp)
            mcp_g[g] = P.op("scalar", lambda s, g=g: s.copy(MASKF[:, g * 128:(g + 1) * 128], PS[0:16, 3584:3712]), inc="act")
        mask_done = mcp_g[NK - 1]
        if debug:
            P.wait("gpsimd", "act", mask_done)
            gdma(dbg_mask, MASKF[:, :])

        passes = [("s", 0), ("s", 1)] + [("e", e) for e in range(NE - SKIP_EXPERTS)]

        def wsrc(kind, idx, which):
            if kind == "s":
                return {"g": dr["wgs_r"], "u": dr["wus_r"], "d": dr["wds_r"]}[which][idx]
            return {"g": dr["wg_r"], "u": dr["wu_r"], "d": dr["wd_r"]}[which][idx]

        load_q = {}
        cons_mark = {}
        nld = [0]

        def emit_load(src):
            n = nld[0]
            nld[0] += 1
            if n >= 2:
                sem, val = cons_mark[n - 2]
                P.wait("gpsimd", sem, val)
            else:
                P.wait("gpsimd", "dve", x1_done)
                P.wait("gpsimd", "pe", mixer_pe_done)
            load_q[n] = gdma(SLOT[n % 2][:, :], src)
            return n

        acc_last = {}
        prev_acc_even = None
        prev_acc_odd = None
        um7_prev = None
        silu7_prev = None
        prev_mb_ev = None
        ln_dve_last = None
        ln_store_last = None
        first_acc_written = False

        for hf in range(2):
            th = hf * HFT
            for j, (kind, idx) in enumerate(passes):
                n_g = emit_load(wsrc(kind, idx, "g"))
                gslot = SLOT[n_g % 2]
                n_u = emit_load(wsrc(kind, idx, "u"))
                uslot = SLOT[n_u % 2]

                if kind == "e":
                    if hf == 0 and j == 2:
                        P.wait("tensor", "act", mask_done)
                    if prev_mb_ev is not None:
                        P.wait("tensor", "act", prev_mb_ev)
                    if ln_dve_last is not None:
                        P.wait("tensor", "dve", ln_dve_last)
                        ln_dve_last = None
                    mb_mm = None
                    for tc in range(2):
                        mb_mm = mm(PS[:, 2048 + tc * 512: 2048 + (tc + 1) * 512],
                                   SELBsb[:, idx * 128:(idx + 1) * 128],
                                   MASKF[:, th + tc * 512: th + (tc + 1) * 512],
                                   True, True, inc="pe" if tc == 1 else None)
                    P.wait("scalar", "pe", mb_mm)
                    if um7_prev is not None:
                        P.wait("scalar", "dve", um7_prev)
                    mb_ev = P.op("scalar", lambda s: s.copy(MBSB[:, :], PS[:, 2048:3072]), inc="act")
                    prev_mb_ev = mb_ev
                else:
                    mb_ev = None

                silu_i = {}
                h_i = {}
                um_i = {}
                g_done_i = {}
                u_done_i = {}
                for i in range(8):
                    pb = (i % 2) * 1024
                    if i == 0:
                        P.wait("tensor", "qg", load_q[n_g])
                        if prev_acc_even is not None:
                            P.wait("tensor", "dve", prev_acc_even)
                    elif i == 1:
                        if prev_acc_odd is not None:
                            P.wait("tensor", "dve", prev_acc_odd)
                    else:
                        P.wait("tensor", "act", silu_i[i - 2])
                    for k in range(NK):
                        for tc in range(2):
                            g_done_i[i] = mm(PS[:, pb + tc * 512: pb + (tc + 1) * 512],
                                             gslot[:, k * 1024 + i * 128: k * 1024 + (i + 1) * 128],
                                             BIG1[:, k * 2048 + th + tc * 512: k * 2048 + th + (tc + 1) * 512],
                                             k == 0, k == NK - 1,
                                             inc="pe" if (k == NK - 1 and tc == 1) else None)
                    P.wait("scalar", "pe", g_done_i[i])
                    if j == 0 and ln_dve_last is not None:
                        if i == 4:
                            P.wait("scalar", "dve", ln_dve_last)
                        if i == 6:
                            P.wait("scalar", "qs", ln_store_last)
                    silu_i[i] = P.op("scalar",
                                     lambda s, i=i, pb=pb: s.activation(HBUF[:, i * 1024:(i + 1) * 1024], PS[:, pb:pb + 1024], AF.Silu),
                                     inc="act")
                cons_mark[n_g] = ("pe", g_done_i[7])
                silu7_prev = silu_i[7]
                for i in range(8):
                    pb = (i % 2) * 1024
                    if i == 0:
                        P.wait("tensor", "qg", load_q[n_u])
                        P.wait("tensor", "act", silu_i[6])
                    elif i == 1:
                        P.wait("tensor", "act", silu_i[7])
                    else:
                        P.wait("tensor", "dve", um_i[i - 2])
                    for k in range(NK):
                        for tc in range(2):
                            u_done_i[i] = mm(PS[:, pb + tc * 512: pb + (tc + 1) * 512],
                                             uslot[:, k * 1024 + i * 128: k * 1024 + (i + 1) * 128],
                                             BIG1[:, k * 2048 + th + tc * 512: k * 2048 + th + (tc + 1) * 512],
                                             k == 0, k == NK - 1,
                                             inc="pe" if (k == NK - 1 and tc == 1) else None)
                    P.wait("vector", "pe", u_done_i[i])
                    if kind == "e":
                        if i == 0:
                            P.wait("vector", "act", mb_ev)
                        um_i[i] = P.op("vector",
                                       lambda v, i=i, pb=pb: v.tensor_tensor(UM[i % 2][:, :], PS[:, pb:pb + 1024], MBSB[:, :], ALU.mult),
                                       inc="dve")
                    else:
                        um_i[i] = P.op("vector",
                                       lambda v, i=i, pb=pb: v.tensor_scalar(UM[i % 2][:, :], PS[:, pb:pb + 1024], RATIO, None, ALU.mult),
                                       inc="dve")
                    P.wait("vector", "act", silu_i[i])
                    if debug and not first_acc_written and i == 0:
                        P.wait("vector", "qg", q_dbg_ys)
                    h_i[i] = P.op("vector",
                                  lambda v, i=i: v.tensor_tensor(HBUF[:, i * 1024:(i + 1) * 1024], HBUF[:, i * 1024:(i + 1) * 1024], UM[i % 2][:, :], ALU.mult),
                                  inc="dve")
                cons_mark[n_u] = ("pe", u_done_i[7])
                um7_prev = um_i[7]

                n_d = emit_load(wsrc(kind, idx, "d"))
                dslot = SLOT[n_d % 2]

                wd_d = {}
                for d in range(NK):
                    if d == 0:
                        P.wait("tensor", "qg", load_q[n_d])
                        P.wait("tensor", "dve", h_i[6])
                    if d >= 2:
                        P.wait("tensor", "dve", acc_last[d - 2])
                    pb = (d % 2) * 1024
                    for k in range(8):
                        if d == 0 and k == 7:
                            P.wait("tensor", "dve", h_i[7])
                        for tc in range(2):
                            wd_d[d] = mm(PS[:, pb + tc * 512: pb + (tc + 1) * 512],
                                         dslot[:, k * 2048 + d * 128: k * 2048 + (d + 1) * 128],
                                         HBUF[:, k * 1024 + tc * 512: k * 1024 + (tc + 1) * 512],
                                         k == 0, k == 7,
                                         inc="pe" if (k == 7 and tc == 1) else None)
                    P.wait("vector", "pe", wd_d[d])
                    if j == 0:
                        acc_last[d] = P.op("vector",
                                           lambda v, d=d, pb=pb, th=th: v.tensor_tensor(
                                               ACC[:, d * 1024:(d + 1) * 1024],
                                               BIG1[:, d * 2048 + th: d * 2048 + th + 1024],
                                               PS[:, pb:pb + 1024], ALU.add),
                                           inc="dve")
                        first_acc_written = True
                    else:
                        acc_last[d] = P.op("vector",
                                           lambda v, d=d, pb=pb: v.tensor_tensor(
                                               ACC[:, d * 1024:(d + 1) * 1024],
                                               ACC[:, d * 1024:(d + 1) * 1024],
                                               PS[:, pb:pb + 1024], ALU.add),
                                           inc="dve")
                cons_mark[n_d] = ("pe", wd_d[NK - 1])
                prev_acc_even = acc_last[14]
                prev_acc_odd = acc_last[15]
                if debug and hf == 0 and j == len(passes) - 1:
                    P.wait("gpsimd", "dve", acc_last[NK - 1])
                    q_hard = gdma(dbg_hb, ACC[:, :])
                    P.wait("gpsimd", "qg", q_hard)

            zqf = {}
            fstat = {}
            for d in range(NK):
                P.wait("scalar", "dve", acc_last[NK - 1] if d == 0 else acc_last[d])
                if d >= 2:
                    P.wait("scalar", "pe", fstat[d - 2])
                zqf[d] = P.op("scalar",
                              lambda s, d=d: s.activation(ZSQF[d % 2][:, :], ACC[:, d * 1024:(d + 1) * 1024], AF.Square),
                              inc="act")
                if d == 0:
                    P.wait("tensor", "dve", acc_last[NK - 1])
                    if prev_mb_ev is not None:
                        P.wait("tensor", "act", prev_mb_ev)
                P.wait("tensor", "act", zqf[d])
                for tc in range(2):
                    mm(PS[0:1, 2048 + tc * 512: 2048 + (tc + 1) * 512], ONESCOLb[:, :],
                       ACC[:, d * 1024 + tc * 512: d * 1024 + (tc + 1) * 512],
                       d == 0, d == NK - 1)
                    fstat[d] = mm(PS[0:1, 3072 + tc * 512: 3072 + (tc + 1) * 512], ONESCOLb[:, :],
                                  ZSQF[d % 2][:, tc * 512:(tc + 1) * 512],
                                  d == 0, d == NK - 1,
                                  inc="pe" if tc == 1 else None)
            P.wait("vector", "pe", fstat[NK - 1])
            def dvf(fn):
                c = P.op("vector", fn, inc="dve")
                P.wait("vector", "dve", c)
                return c
            dvf(lambda v: v.tensor_scalar(FR1[:, :], PS[0:1, 2048:3072], 1.0 / D, None, ALU.mult))
            dvf(lambda v: v.tensor_scalar(FR2[:, :], PS[0:1, 3072:4096], 1.0 / D, None, ALU.mult))
            dvf(lambda v: v.tensor_tensor(PS[0:1, 2048:3072], FR1[:, :], FR1[:, :], ALU.mult))
            dvf(lambda v: v.tensor_tensor(PS[0:1, 3072:4096], FR2[:, :], PS[0:1, 2048:3072], ALU.subtract))
            f_var = dvf(lambda v: v.tensor_scalar(PS[0:1, 3072:4096], PS[0:1, 3072:4096], LN_EPS, None, ALU.add))
            P.wait("scalar", "dve", f_var)
            f_ln = P.op("scalar", lambda s: s.activation(PS[0:1, 2048:3072], PS[0:1, 3072:4096], AF.Ln), inc="act")
            P.wait("vector", "act", f_ln)
            f_lh = dvf(lambda v: v.tensor_scalar(PS[0:1, 2048:3072], PS[0:1, 2048:3072], -0.5, None, ALU.mult))
            P.wait("scalar", "dve", f_lh)
            f_rstd = P.op("scalar", lambda s: s.activation(FR2[:, :], PS[0:1, 2048:3072], AF.Exp), inc="act")
            P.wait("vector", "act", f_rstd)
            f_mr = dvf(lambda v: v.tensor_tensor(FR1[:, :], FR1[:, :], FR2[:, :], ALU.mult))
            P.wait("tensor", "dve", f_mr)
            fbc = None
            for tc in range(2):
                mm(PS[:, 2048 + tc * 512: 2048 + (tc + 1) * 512], ONES1b[:, :],
                   FR2[0:1, tc * 512:(tc + 1) * 512], True, True)
                fbc = mm(PS[:, 3072 + tc * 512: 3072 + (tc + 1) * 512], ONES1b[:, :],
                         FR1[0:1, tc * 512:(tc + 1) * 512], True, True,
                         inc="pe" if tc == 1 else None)
            st_q = {}
            out_d = {}
            P.wait("vector", "pe", fbc)
            for d in range(NK):
                if d >= 2:
                    P.wait("vector", "qs", st_q[d - 2])
                P.op("vector", lambda v, d=d: v.tensor_tensor(
                    T1F[:, :], ACC[:, d * 1024:(d + 1) * 1024], PS[:, 2048:3072], ALU.mult),
                    inc="dve")
                out_d[d] = P.op("vector", lambda v, d=d: v.tensor_tensor(
                    OST[d % 2][:, :], T1F[:, :], PS[:, 3072:4096], ALU.subtract),
                    inc="dve")
                P.wait("sync", "dve", out_d[d])
                st_q[d] = sdma(outT[d * 128:(d + 1) * 128, th:th + HFT], OST[d % 2][:, :])
            P.wait("vector", "qs", st_q[NK - 1])
            ln_dve_last = out_d[NK - 1]
            ln_store_last = st_q[NK - 1]

        P.wait("sync", "qs", P.cnt["qs"])
        if debug:
            P.wait("gpsimd", "qg", P.cnt["qg"])

        P.check()

        with nc.Block() as block:
            block.tensor(lambda t: P.replay(t, "tensor"))
            block.vector(lambda v: P.replay(v, "vector"))
            block.scalar(lambda s: P.replay(s, "scalar"))
            block.gpsimd(lambda g: P.replay(g, "gpsimd"))
            block.sync(lambda s: P.replay(s, "sync"))

    return nc


def _rearr(a, kp=128):
    K = a.shape[0] // kp
    return np.ascontiguousarray(a.reshape(K, kp, a.shape[1]).transpose(1, 0, 2).reshape(kp, K * a.shape[1]))


def kernel(L0_attn_x, L0_attn_W_in, L0_attn_W_out, L0_attn_log_A, L0_attn_W_dt,
           L0_attn_b_dt, L0_ffn_centroids, L0_ffn_gate_bias, L0_ffn_wg_s,
           L0_ffn_wu_s, L0_ffn_wd_s, L0_ffn_wg, L0_ffn_wu, L0_ffn_wd):
    f = np.float32
    x = np.asarray(L0_attn_x, f)
    W_in = np.asarray(L0_attn_W_in, f)
    W_out = np.asarray(L0_attn_W_out, f)
    log_A = np.asarray(L0_attn_log_A, f)
    W_dt = np.asarray(L0_attn_W_dt, f)
    b_dt = np.asarray(L0_attn_b_dt, f)
    cent = np.asarray(L0_ffn_centroids, f)
    gb = np.asarray(L0_ffn_gate_bias, f)
    wg_s = np.asarray(L0_ffn_wg_s, f)
    wu_s = np.asarray(L0_ffn_wu_s, f)
    wd_s = np.asarray(L0_ffn_wd_s, f)
    wg = np.asarray(L0_ffn_wg, f)
    wu = np.asarray(L0_ffn_wu, f)
    wd = np.asarray(L0_ffn_wd, f)

    winT_r = np.ascontiguousarray(
        W_in.T.reshape(16, 128, 16, 128).transpose(1, 2, 0, 3).reshape(128, 32768))
    woutT = W_out.T
    woutT_q = np.stack([_rearr(np.ascontiguousarray(woutT[:, q * 512:(q + 1) * 512]))
                        for q in range(4)])
    wdtT_r = _rearr(np.ascontiguousarray(W_dt.T))
    centT_r = _rearr(np.ascontiguousarray(cent.T))
    logA_r = np.ascontiguousarray(log_A.reshape(-1).reshape(16, 128).T)
    bdt_c = np.ascontiguousarray(b_dt.reshape(32, 1))
    gb_row = np.ascontiguousarray(gb.reshape(1, 16))
    wg_r = np.stack([_rearr(wg[e]) for e in range(NE)])
    wu_r = np.stack([_rearr(wu[e]) for e in range(NE)])
    wd_r = np.stack([_rearr(wd[e]) for e in range(NE)])
    wgs_r = np.stack([_rearr(np.ascontiguousarray(wg_s[:, s * 1024:(s + 1) * 1024])) for s in range(2)])
    wus_r = np.stack([_rearr(np.ascontiguousarray(wu_s[:, s * 1024:(s + 1) * 1024])) for s in range(2)])
    wds_r = np.stack([_rearr(np.ascontiguousarray(wd_s[s * 1024:(s + 1) * 1024, :])) for s in range(2)])
    SEL = np.kron(np.eye(32, dtype=f), np.ones((1, 64), f))
    SELB = np.kron(np.eye(16, dtype=f), np.ones((1, 128), f))
    EYE = np.eye(128, dtype=f)
    ONESCOL = np.ones((128, 1), f)
    ONES1 = np.ones((1, 128), f)

    shared = {
        "winT_r": winT_r, "woutT_q": woutT_q, "wdtT_r": wdtT_r, "centT_r": centT_r,
        "logA_r": logA_r, "bdt_c": bdt_c, "gb_row": gb_row,
        "wg_r": wg_r, "wu_r": wu_r, "wd_r": wd_r,
        "wgs_r": wgs_r, "wus_r": wus_r, "wds_r": wds_r,
        "SEL": SEL, "SELB": SELB, "EYE": EYE, "ONESCOL": ONESCOL, "ONES1": ONES1,
    }
    in_maps = []
    for c in range(8):
        m = dict(shared)
        m["xT_r"] = _rearr(np.ascontiguousarray(x[c].T))
        in_maps.append(m)

    nc = build_program(debug=DEBUG)
    res = run_bass_kernel_spmd(nc, in_maps, core_ids=list(range(8)))
    out = np.stack([np.ascontiguousarray(r["outT"].T) for r in res.results])
    if DEBUG:
        kernel._debug = res.results
    return out.astype(np.float32)


# version 09
# speedup vs baseline: 1.1335x; 1.0379x over previous
import numpy as np
import concourse.bass as bass
import concourse.mybir as mybir
from concourse.bass_utils import run_bass_kernel_spmd

F32 = mybir.dt.float32
BF16 = mybir.dt.bfloat16
AF = mybir.ActivationFunctionType
ALU = mybir.AluOpType
AX = mybir.AxisListType

D = 2048
T = 2048
NK = 16
NE = 16
HFT = 1024
LN_EPS = 1e-5
MOE_EPS = 1e-9
RATIO = 0.1

DEBUG = False
SKIP_EXPERTS = 0

ENGS = ("tensor", "vector", "scalar", "gpsimd", "sync")


class Prog:
    def __init__(self):
        self.ops = {e: [] for e in ENGS}
        self.sems = {}
        self.cnt = {}

    def add_sem(self, name, handle):
        self.sems[name] = handle
        self.cnt[name] = 0

    def op(self, eng, fn, inc=None, amt=1):
        if inc is None:
            self.ops[eng].append((fn, None, 0, ("op", None, 0)))
            return None
        self.cnt[inc] += amt
        self.ops[eng].append((fn, self.sems[inc], amt, ("inc", inc, amt)))
        return self.cnt[inc]

    def wait(self, eng, sem, val):
        if val is None or val <= 0:
            return
        h = self.sems[sem]
        self.ops[eng].append((lambda e, h=h, v=val: e.wait_ge(h, v), None, 0,
                              ("wait", sem, val)))

    def replay(self, eng_obj, eng_name):
        for fn, semh, amt, _meta in self.ops[eng_name]:
            instr = fn(eng_obj)
            if semh is not None:
                instr.then_inc(semh, amt)

    def check(self):
        counts = {s: 0 for s in self.cnt}
        ptrs = {e: 0 for e in self.ops}
        total = sum(len(v) for v in self.ops.values())
        done = 0
        while done < total:
            progress = False
            for e in self.ops:
                lst = self.ops[e]
                while ptrs[e] < len(lst):
                    _fn, _h, _amt, meta = lst[ptrs[e]]
                    if meta[0] == "wait" and counts[meta[1]] < meta[2]:
                        break
                    if meta[0] == "inc":
                        counts[meta[1]] += meta[2]
                    ptrs[e] += 1
                    done += 1
                    progress = True
            if not progress:
                blocked = {e: (ptrs[e], self.ops[e][ptrs[e]][3] if ptrs[e] < len(self.ops[e]) else None)
                           for e in self.ops if ptrs[e] < len(self.ops[e])}
                raise RuntimeError(f"DEADLOCK: counts={counts} blocked={blocked}")
        return True


def build_program(debug=False):
    nc = bass.Bass("TRN2", target_bir_lowering=False)

    dr = {}

    def din(name, shape, dt=F32):
        dr[name] = nc.dram_tensor(name, shape, dt, kind="ExternalInput").ap()

    din("xT_r", [128, NK * T])
    din("winT_r", [128, NK * D])
    din("woutT_q", [4, 128, NK * 512])
    din("wdtT_r", [128, NK * 32])
    din("centT_r", [128, NK * 16])
    din("logA_r", [128, 16])
    din("bdt_c", [32, 1])
    din("gb_row", [1, 16])
    din("wg_r", [NE, 128, NK * 1024])
    din("wu_r", [NE, 128, NK * 1024])
    din("wd_r", [NE, 128, 8 * D])
    din("wgs_r", [2, 128, NK * 1024])
    din("wus_r", [2, 128, NK * 1024])
    din("wds_r", [2, 128, 8 * D])
    din("SEL", [32, D])
    din("SELB", [16, D])
    din("EYE", [128, 128])
    din("ONESCOL", [128, 1])
    din("ONES1", [1, 128])

    outT = nc.dram_tensor("outT", [D, T], F32, kind="ExternalOutput").ap()
    if debug:
        dbg_xp = nc.dram_tensor("dbg_xp", [128, NK * T], F32, kind="ExternalOutput").ap()
        dbg_ys = nc.dram_tensor("dbg_ys", [128, NK * T], F32, kind="ExternalOutput").ap()
        dbg_x1 = nc.dram_tensor("dbg_x1", [128, NK * T], F32, kind="ExternalOutput").ap()
        dbg_mask = nc.dram_tensor("dbg_mask", [16, T], F32, kind="ExternalOutput").ap()
        dbg_dt = nc.dram_tensor("dbg_dt", [32, T], F32, kind="ExternalOutput").ap()
        dbg_aff = nc.dram_tensor("dbg_aff", [128, 16], F32, kind="ExternalOutput").ap()
        dbg_btok = nc.dram_tensor("dbg_btok", [128, 16], F32, kind="ExternalOutput").ap()
        dbg_m1 = nc.dram_tensor("dbg_m1", [128, 1], F32, kind="ExternalOutput").ap()
        dbg_t1r = nc.dram_tensor("dbg_t1r", [128, 16], F32, kind="ExternalOutput").ap()
        dbg_mskt = nc.dram_tensor("dbg_mskt", [128, 16], F32, kind="ExternalOutput").ap()
        dbg_acc = nc.dram_tensor("dbg_acc", [128, 16384], F32, kind="ExternalOutput").ap()
        dbg_hb = nc.dram_tensor("dbg_hb", [128, 16384], F32, kind="ExternalOutput").ap()

    SB0 = ((nc.sbuf_base + 63) // 64) * 64

    def at(name, shape, dt, off):
        return nc.alloc_sbuf_tensor_at(name, shape, dt, offset=SB0 + off).ap()

    KB = 1024
    BIG1 = at("BIG1", [128, 32768], BF16, 0)
    BIG2 = at("BIG2", [128, 32768], BF16, 64 * KB)
    A0 = 128 * KB
    WIB = at("WIB", [128, 32768], BF16, A0)
    SELsb = at("SELsb", [32, 2048], BF16, A0)
    DTS = at("DTS", [32, 2048], BF16, A0 + 4 * KB)
    WDTb = at("WDTb", [128, 512], BF16, A0 + 8 * KB)
    XRB = at("XRB", [128, 2048], BF16, A0 + 12 * KB)
    ASB = [at("ASB0", [128, 2048], F32, A0 + 12 * KB),
           at("ASB1", [128, 2048], F32, A0 + 20 * KB)]
    ZSQ = [at("ZSQ0", [128, 2048], BF16, A0 + 16 * KB),
           at("ZSQ1", [128, 2048], BF16, A0 + 20 * KB)]
    T1 = at("T1", [128, 2048], BF16, A0 + 28 * KB)
    RBSB = at("RBSB", [128, 2048], BF16, A0 + 16 * KB)
    MRSB = at("MRSB", [128, 2048], BF16, A0 + 20 * KB)
    R1 = at("R1", [1, 2048], BF16, A0 + 24 * KB)
    R2 = at("R2", [1, 2048], BF16, A0 + 28 * KB)
    WOBQ = [at("WOBQ0", [128, 8192], BF16, A0 + 32 * KB),
            at("WOBQ1", [128, 8192], BF16, A0 + 48 * KB)]
    SLOT = [at("SLOTA", [128, 16384], BF16, A0),
            at("SLOTB", [128, 16384], BF16, A0 + 32 * KB)]
    B2 = 64 * KB
    ACC = at("ACC", [128, 16384], BF16, B2)
    HBUF = at("HBUF", [128, 8192], BF16, B2 + 32 * KB)
    AFFT = at("AFFT", [128, 256], F32, B2 + 48 * KB)
    BTOK = at("BTOK", [128, 256], F32, B2 + 49 * KB)
    B2R = at("B2R", [128, 256], F32, B2 + 50 * KB)
    T1R = at("T1R", [128, 256], F32, B2 + 51 * KB)
    T2R = at("T2R", [128, 256], F32, B2 + 62 * KB)
    WSEL = at("WSEL", [128, 256], F32, B2 + 63 * KB)
    XSTG = at("XSTG", [128, 16384], F32, 0)
    UM = [at("UM0", [128, 1024], BF16, B2 + 52 * KB),
          at("UM1", [128, 1024], BF16, B2 + 54 * KB)]
    MBSB = at("MBSB", [128, 1024], BF16, B2 + 56 * KB)
    OST = [at("OST0", [128, 1024], F32, B2 + 58 * KB),
           at("OST1", [128, 1024], F32, B2 + 44 * KB)]
    ZSQF = [at("ZSQF0", [128, 1024], BF16, B2 + 32 * KB),
            at("ZSQF1", [128, 1024], BF16, B2 + 34 * KB)]
    FR1 = at("FR1", [1, 1024], BF16, B2 + 36 * KB)
    FR2 = at("FR2", [1, 1024], BF16, B2 + 38 * KB)
    T1F = at("T1F", [128, 1024], F32, B2 + 40 * KB)

    S0 = 192 * KB
    SELBsb = at("SELBsb", [16, 2048], BF16, S0)
    MASKF = at("MASKF", [16, 2048], BF16, S0 + 4 * KB)
    CTBb = at("CTBb", [128, 256], BF16, S0 + 8 * KB)
    EYEf = at("EYEf", [128, 128], F32, S0 + 8 * KB + 512)
    ONESCOLb = at("ONESCOLb", [128, 1], BF16, S0 + 9 * KB)
    ONES1b = at("ONES1b", [1, 128], BF16, S0 + 9 * KB + 64)
    ONES1f = at("ONES1f", [1, 128], F32, S0 + 9 * KB + 384)
    LOGR = at("LOGR", [128, 16], F32, S0 + 10 * KB)
    ACsb = at("ACsb", [128, 16], F32, S0 + 10 * KB + 64)
    BDT = at("BDT", [32, 1], F32, S0 + 10 * KB + 128)
    GBROW = at("GBROW", [1, 16], F32, S0 + 10 * KB + 192)
    GBBC = at("GBBC", [128, 16], F32, S0 + 10 * KB + 256)
    MSKT = at("MSKT", [128, 256], F32, S0 + 10 * KB + 320)
    M1R = at("M1R", [128, 16], F32, S0 + 11 * KB + 384)
    DENR = at("DENR", [128, 16], F32, S0 + 11 * KB + 448)
    DINV = at("DINV", [128, 16], F32, S0 + 11 * KB + 512)

    P = Prog()

    import contextlib
    stack = contextlib.ExitStack()
    with stack:
        PS = stack.enter_context(nc.psum_tensor("PS", [128, 4096], F32))
        for s in ("qg", "qs", "pe", "act", "dve"):
            P.add_sem(s, stack.enter_context(nc.semaphore(s)))

        def gdma(dst, src):
            return P.op("gpsimd", lambda g, d=dst, s=src: g.dma_start(out=d, in_=s),
                        inc="qg", amt=16)

        def sdma(dst, src):
            return P.op("sync", lambda g, d=dst, s=src: g.dma_start(out=d, in_=s),
                        inc="qs", amt=16)

        def mm(out, lhsT, rhs, start, stop, inc=None):
            return P.op("tensor",
                        lambda t, o=out, l=lhsT, r=rhs, st=start, sp=stop:
                        t.matmul(o, l, r, start=st, stop=sp),
                        inc=inc)

        q_xb = gdma(BIG2[:, 0:16384], dr["xT_r"][:, 0:16384])
        q_win = []
        for e in range(NK):
            q_win.append(gdma(WIB[:, e * 2048:(e + 1) * 2048],
                              dr["winT_r"][:, e * 2048:(e + 1) * 2048]))
        q_const = []
        q_const.append(gdma(SELBsb[:, :], dr["SELB"]))
        q_const.append(gdma(CTBb[:, :], dr["centT_r"]))
        q_const.append(gdma(EYEf[:, :], dr["EYE"]))
        q_const.append(gdma(ONESCOLb[:, :], dr["ONESCOL"]))
        q_const.append(gdma(ONES1b[:, :], dr["ONES1"]))
        q_const.append(gdma(ONES1f[:, :], dr["ONES1"]))
        q_const.append(gdma(LOGR[:, :], dr["logA_r"]))
        q_const.append(gdma(BDT[:, :], dr["bdt_c"]))
        q_const.append(gdma(GBROW[:, :], dr["gb_row"]))
        q_x2 = sdma(XSTG[:, :], dr["xT_r"][:, 16384:32768])
        P.wait("scalar", "qs", q_x2)
        c_x2 = P.op("scalar", lambda s: s.copy(BIG2[:, 16384:32768], XSTG[:, :]), inc="act")

        P.wait("scalar", "qg", q_const[8])
        c_exp = P.op("scalar", lambda s: s.activation(ACsb[:, :], LOGR[:, :], AF.Exp),
                     inc="act")
        P.wait("vector", "act", c_exp)
        c_neg = P.op("vector",
                     lambda v: v.tensor_scalar(ACsb[:, :], ACsb[:, :], -1.0, None, ALU.mult),
                     inc="dve")
        P.wait("tensor", "qg", q_const[8])
        c_gb = mm(PS[:, 3584:3600], ONES1f[:, :], GBROW[:, :], True, True, inc="pe")
        P.wait("vector", "pe", c_gb)
        P.op("vector", lambda v: v.tensor_copy(GBBC[:, :], PS[:, 3584:3600]), inc="dve")

        xp_ev = {}
        P.wait("tensor", "qg", q_xb)
        P.wait("tensor", "act", c_x2)
        for e in range(NK):
            pb = (e % 2) * 2048
            P.wait("tensor", "qg", q_win[e])
            if e >= 2:
                P.wait("tensor", "act", xp_ev[e - 2])
            xp_mm = None
            for k in range(NK):
                for tc in range(4):
                    xp_mm = mm(PS[:, pb + tc * 512: pb + (tc + 1) * 512],
                               WIB[:, e * 2048 + k * 128: e * 2048 + (k + 1) * 128],
                               BIG2[:, k * 2048 + tc * 512: k * 2048 + (tc + 1) * 512],
                               k == 0, k == NK - 1,
                               inc="pe" if (k == NK - 1 and tc == 3) else None)
            P.wait("scalar", "pe", xp_mm)
            xp_ev[e] = P.op("scalar",
                            lambda s, e=e, pb=pb: s.copy(BIG1[:, e * 2048:(e + 1) * 2048],
                                                         PS[:, pb:pb + 2048]),
                            inc="act")
        xp_done_pe = P.cnt["pe"]
        xp_ev_all = xp_ev[NK - 1]

        if debug:
            P.wait("gpsimd", "act", xp_ev_all)
            q_dbg_xp = gdma(dbg_xp, BIG1[:, :])

        P.wait("gpsimd", "pe", xp_done_pe)
        q_sel = gdma(SELsb[:, :], dr["SEL"])
        q_wdt = gdma(WDTb[:, :], dr["wdtT_r"])
        q_wob = [gdma(WOBQ[0][:, :], dr["woutT_q"][0]),
                 gdma(WOBQ[1][:, :], dr["woutT_q"][1])]

        P.wait("tensor", "qg", q_wdt)
        P.wait("tensor", "act", xp_ev_all)
        dt_mm = None
        for k in range(NK):
            for tc in range(4):
                dt_mm = mm(PS[0:32, tc * 512:(tc + 1) * 512],
                           WDTb[:, k * 32:(k + 1) * 32],
                           BIG1[:, k * 2048 + tc * 512: k * 2048 + (tc + 1) * 512],
                           k == 0, k == NK - 1,
                           inc="pe" if (k == NK - 1 and tc == 3) else None)
        P.wait("scalar", "pe", dt_mm)
        P.op("scalar", lambda s: s.activation(PS[0:32, 2048:4096], PS[0:32, 0:2048],
                                              AF.Exp, bias=BDT[:, 0:1]), inc="act")
        dts_done = P.op("scalar", lambda s: s.activation(DTS[:, :], PS[0:32, 2048:4096],
                                                         AF.Ln, bias=1.0), inc="act")
        if debug:
            P.wait("gpsimd", "act", dts_done)
            gdma(dbg_dt, DTS[:, :])

        scan_done = {}
        aexp = {}
        P.wait("tensor", "act", dts_done)
        P.wait("tensor", "qg", q_sel)
        P.wait("scalar", "dve", c_neg)
        for ct in range(NK):
            if ct >= 1:
                P.wait("tensor", "act", aexp[ct - 1])
            sel_mm = None
            for tc in range(4):
                sel_mm = mm(PS[:, 2048 + tc * 512: 2048 + (tc + 1) * 512],
                            SELsb[:, ct * 128:(ct + 1) * 128],
                            DTS[:, tc * 512:(tc + 1) * 512],
                            True, True, inc="pe" if tc == 3 else None)
            P.wait("scalar", "pe", sel_mm)
            if ct >= 2:
                P.wait("scalar", "dve", scan_done[ct - 2])
            aexp[ct] = P.op("scalar",
                            lambda s, ct=ct: s.activation(ASB[ct % 2][:, :], PS[:, 2048:4096],
                                                          AF.Exp, scale=ACsb[:, ct:ct + 1]),
                            inc="act")
            P.wait("vector", "act", aexp[ct])
            scan_done[ct] = P.op("vector",
                                 lambda v, ct=ct: v.tensor_tensor_scan(
                                     BIG2[:, ct * 2048:(ct + 1) * 2048],
                                     ASB[ct % 2][:, :],
                                     BIG1[:, ct * 2048:(ct + 1) * 2048],
                                     0.0, ALU.mult, ALU.add),
                                 inc="dve")
            if ct == 0:
                P.wait("tensor", "act", dts_done)
                P.wait("tensor", "qg", q_wob[0])
            for kc in ([ct - 1] if ct >= 1 else []) + ([NK - 1] if ct == NK - 1 else []):
                P.wait("tensor", "dve", scan_done[kc])
                for tc in range(4):
                    y0_mm = mm(PS[:, tc * 512:(tc + 1) * 512],
                               WOBQ[0][:, kc * 512: kc * 512 + 128],
                               BIG2[:, kc * 2048 + tc * 512: kc * 2048 + (tc + 1) * 512],
                               kc == 0, kc == NK - 1,
                               inc="pe" if (kc == NK - 1 and tc == 3) else None)
        if debug:
            P.wait("gpsimd", "dve", scan_done[NK - 1])
            q_dbg_ys = gdma(dbg_ys, BIG2[:, :])

        z_done = {}
        y_done = {}
        y_done[0] = y0_mm
        wobq_next = 2
        for e in range(NK):
            if e >= 1:
                P.wait("gpsimd", "dve", z_done[e - 1])
            else:
                P.wait("gpsimd", "dve", scan_done[NK - 1])
            xrb_q = gdma(XRB[:, :], dr["xT_r"][:, e * 2048:(e + 1) * 2048])
            q = e // 4
            pb = (e % 2) * 2048
            if e >= 1:
                P.wait("tensor", "qg", q_wob[q])
            if e == 1:
                P.wait("tensor", "act", aexp[NK - 1])
            if e >= 2:
                P.wait("tensor", "dve", z_done[e - 2])
            for k in range(NK):
                for tc in range(4):
                    if e == 0:
                        break
                    y_done[e] = mm(PS[:, pb + tc * 512: pb + (tc + 1) * 512],
                                   WOBQ[q % 2][:, k * 512 + (e % 4) * 128: k * 512 + (e % 4 + 1) * 128],
                                   BIG2[:, k * 2048 + tc * 512: k * 2048 + (tc + 1) * 512],
                                   k == 0, k == NK - 1,
                                   inc="pe" if (k == NK - 1 and tc == 3) else None)
            if e % 4 == 3 and wobq_next < 4:
                P.wait("gpsimd", "pe", y_done[e])
                q_wob.append(gdma(WOBQ[wobq_next % 2][:, :], dr["woutT_q"][wobq_next]))
                wobq_next += 1
            P.wait("vector", "pe", y_done[e])
            P.wait("vector", "qg", xrb_q)
            if debug and e == 0:
                P.wait("vector", "qg", q_dbg_xp)
            z_done[e] = P.op("vector",
                             lambda v, e=e, pb=pb: v.tensor_tensor(
                                 BIG1[:, e * 2048:(e + 1) * 2048],
                                 XRB[:, :], PS[:, pb:pb + 2048], ALU.add),
                             inc="dve")

        zq = {}
        stat_mm = {}
        for e in range(NK):
            P.wait("scalar", "dve", z_done[e])
            if e >= 2:
                P.wait("scalar", "pe", stat_mm[e - 2])
            zq[e] = P.op("scalar",
                         lambda s, e=e: s.activation(ZSQ[e % 2][:, :],
                                                     BIG1[:, e * 2048:(e + 1) * 2048],
                                                     AF.Square),
                         inc="act")
            if e == 0:
                P.wait("tensor", "dve", z_done[NK - 1])
                P.wait("tensor", "act", aexp[NK - 1])
            P.wait("tensor", "act", zq[e])
            for tc in range(4):
                mm(PS[0:1, tc * 512:(tc + 1) * 512], ONESCOLb[:, :],
                   BIG1[:, e * 2048 + tc * 512: e * 2048 + (tc + 1) * 512],
                   e == 0, e == NK - 1)
                stat_mm[e] = mm(PS[0:1, 2048 + tc * 512: 2048 + (tc + 1) * 512], ONESCOLb[:, :],
                                ZSQ[e % 2][:, tc * 512:(tc + 1) * 512],
                                e == 0, e == NK - 1,
                                inc="pe" if tc == 3 else None)
        P.wait("vector", "pe", stat_mm[NK - 1])
        def dvr(fn):
            c = P.op("vector", fn, inc="dve")
            P.wait("vector", "dve", c)
            return c
        dvr(lambda v: v.tensor_scalar(R1[:, :], PS[0:1, 0:2048], 1.0 / D, None, ALU.mult))
        dvr(lambda v: v.tensor_scalar(R2[:, :], PS[0:1, 2048:4096], 1.0 / D, None, ALU.mult))
        dvr(lambda v: v.tensor_tensor(PS[0:1, 0:2048], R1[:, :], R1[:, :], ALU.mult))
        dvr(lambda v: v.tensor_tensor(PS[0:1, 2048:4096], R2[:, :], PS[0:1, 0:2048], ALU.subtract))
        r_var = dvr(lambda v: v.tensor_scalar(PS[0:1, 2048:4096], PS[0:1, 2048:4096], LN_EPS, None, ALU.add))
        P.wait("scalar", "dve", r_var)
        r_ln = P.op("scalar", lambda s: s.activation(PS[0:1, 0:2048], PS[0:1, 2048:4096], AF.Ln), inc="act")
        P.wait("vector", "act", r_ln)
        r_lh = dvr(lambda v: v.tensor_scalar(PS[0:1, 0:2048], PS[0:1, 0:2048], -0.5, None, ALU.mult))
        P.wait("scalar", "dve", r_lh)
        r_rstd = P.op("scalar", lambda s: s.activation(R2[:, :], PS[0:1, 0:2048], AF.Exp), inc="act")
        P.wait("vector", "act", r_rstd)
        r_mr = dvr(lambda v: v.tensor_tensor(R1[:, :], R1[:, :], R2[:, :], ALU.mult))
        P.wait("tensor", "dve", r_mr)
        bc_mm = None
        for tc in range(4):
            mm(PS[:, tc * 512:(tc + 1) * 512], ONES1b[:, :],
               R2[0:1, tc * 512:(tc + 1) * 512], True, True)
            bc_mm = mm(PS[:, 2048 + tc * 512: 2048 + (tc + 1) * 512], ONES1b[:, :],
                       R1[0:1, tc * 512:(tc + 1) * 512], True, True,
                       inc="pe" if tc == 3 else None)
        mixer_pe_done = bc_mm
        P.wait("scalar", "pe", bc_mm)
        P.op("scalar", lambda s: s.copy(RBSB[:, :], PS[:, 0:2048]), inc="act")
        bc_ev = P.op("scalar", lambda s: s.copy(MRSB[:, :], PS[:, 2048:4096]), inc="act")
        ap_done = {}
        P.wait("vector", "act", bc_ev)
        for e in range(NK):
            P.op("vector", lambda v, e=e: v.tensor_tensor(
                T1[:, :], BIG1[:, e * 2048:(e + 1) * 2048], RBSB[:, :], ALU.mult),
                inc="dve")
            ap_done[e] = P.op("vector", lambda v, e=e: v.tensor_tensor(
                BIG1[:, e * 2048:(e + 1) * 2048], T1[:, :], MRSB[:, :], ALU.subtract),
                inc="dve")
        x1_done = ap_done[NK - 1]
        if debug:
            P.wait("gpsimd", "dve", x1_done)
            gdma(dbg_x1, BIG1[:, :])

        P.wait("tensor", "dve", x1_done)
        aff_mm = None
        for g in range(NK):
            for k in range(NK):
                aff_mm = mm(PS[:, 3072 + g * 16: 3072 + (g + 1) * 16],
                            BIG1[:, k * 2048 + g * 128: k * 2048 + (g + 1) * 128],
                            CTBb[:, k * 16:(k + 1) * 16],
                            k == 0, k == NK - 1,
                            inc="pe" if (g == NK - 1 and k == NK - 1) else None)
        P.wait("scalar", "pe", aff_mm)
        sig_all = P.op("scalar", lambda s: s.activation(AFFT[:, :], PS[:, 3072:3328], AF.Sigmoid), inc="act")
        P.wait("vector", "act", sig_all)

        def g3(apv):
            return apv.rearrange("p (g e) -> p g e", e=16)

        def bc(apv):
            return apv.unsqueeze(-1).broadcast_to([128, 16, 16])

        def dv(fn):
            c = P.op("vector", fn, inc="dve")
            P.wait("vector", "dve", c)
            return c
        dv(lambda v: v.tensor_tensor(g3(BTOK[:, :]), g3(AFFT[:, :]), GBBC[:, :].unsqueeze(1).broadcast_to([128, 16, 16]), ALU.add))
        dv(lambda v: v.reduce_max(M1R[:, :], g3(BTOK[:, :]), axis=AX.X))
        dv(lambda v: v.tensor_tensor(g3(T1R[:, :]), g3(BTOK[:, :]), bc(M1R[:, :]), ALU.is_ge))
        dv(lambda v: v.scalar_tensor_tensor(B2R[:, :], T1R[:, :], -1e9, BTOK[:, :], op0=ALU.mult, op1=ALU.add))
        dv(lambda v: v.reduce_max(M1R[:, :], g3(B2R[:, :]), axis=AX.X))
        dv(lambda v: v.tensor_tensor(g3(T2R[:, :]), g3(B2R[:, :]), bc(M1R[:, :]), ALU.is_ge))
        dv(lambda v: v.tensor_tensor(T1R[:, :], T1R[:, :], T2R[:, :], ALU.add))
        dv(lambda v: v.tensor_tensor(WSEL[:, :], AFFT[:, :], T1R[:, :], ALU.mult))
        dv(lambda v: v.reduce_sum(DENR[:, :], g3(WSEL[:, :]), axis=AX.X))
        dv(lambda v: v.tensor_scalar(DENR[:, :], DENR[:, :], MOE_EPS, None, ALU.add))
        dv(lambda v: v.reciprocal(DINV[:, :], DENR[:, :]))
        rt_all = dv(lambda v: v.tensor_tensor(g3(MSKT[:, :]), g3(WSEL[:, :]), bc(DINV[:, :]), ALU.mult))
        if debug:
            P.wait("gpsimd", "dve", rt_all)
            gdma(dbg_aff, AFFT[:, 0:16])
            gdma(dbg_btok, BTOK[:, 0:16])
            gdma(dbg_m1, M1R[:, 0:1])
            gdma(dbg_t1r, T1R[:, 0:16])
            dbg_rt_q = gdma(dbg_mskt, MSKT[:, 0:16])
        mcp_g = {}
        P.wait("tensor", "dve", rt_all)
        for g in range(NK):
            if g >= 1:
                P.wait("tensor", "act", mcp_g[g - 1])
            tp = P.op("tensor", lambda t, g=g: t.transpose(PS[0:16, 3584:3712], MSKT[:, g * 16:(g + 1) * 16], EYEf[:, :]), inc="pe")
            P.wait("scalar", "pe", tp)
            mcp_g[g] = P.op("scalar", lambda s, g=g: s.copy(MASKF[:, g * 128:(g + 1) * 128], PS[0:16, 3584:3712]), inc="act")
        mask_done = mcp_g[NK - 1]
        if debug:
            P.wait("gpsimd", "act", mask_done)
            gdma(dbg_mask, MASKF[:, :])

        passes = [("s", 0), ("s", 1)] + [("e", e) for e in range(NE - SKIP_EXPERTS)]

        def wsrc(kind, idx, which):
            if kind == "s":
                return {"g": dr["wgs_r"], "u": dr["wus_r"], "d": dr["wds_r"]}[which][idx]
            return {"g": dr["wg_r"], "u": dr["wu_r"], "d": dr["wd_r"]}[which][idx]

        load_q = {}
        cons_mark = {}
        nld = [0]

        def emit_load(src):
            n = nld[0]
            nld[0] += 1
            if n >= 2:
                sem, val = cons_mark[n - 2]
                P.wait("gpsimd", sem, val)
            else:
                P.wait("gpsimd", "dve", x1_done)
                P.wait("gpsimd", "pe", mixer_pe_done)
            load_q[n] = gdma(SLOT[n % 2][:, :], src)
            return n

        acc_last = {}
        prev_acc_even = None
        prev_acc_odd = None
        um7_prev = None
        silu7_prev = None
        prev_mb_ev = None
        ln_dve_last = None
        ln_store_last = None
        first_acc_written = False

        for hf in range(2):
            th = hf * HFT
            for j, (kind, idx) in enumerate(passes):
                n_g = emit_load(wsrc(kind, idx, "g"))
                gslot = SLOT[n_g % 2]
                n_u = emit_load(wsrc(kind, idx, "u"))
                uslot = SLOT[n_u % 2]

                if kind == "e":
                    if hf == 0 and j == 2:
                        P.wait("tensor", "act", mask_done)
                    if prev_mb_ev is not None:
                        P.wait("tensor", "act", prev_mb_ev)
                    if ln_dve_last is not None:
                        P.wait("tensor", "dve", ln_dve_last)
                        ln_dve_last = None
                    mb_mm = None
                    for tc in range(2):
                        mb_mm = mm(PS[:, 2048 + tc * 512: 2048 + (tc + 1) * 512],
                                   SELBsb[:, idx * 128:(idx + 1) * 128],
                                   MASKF[:, th + tc * 512: th + (tc + 1) * 512],
                                   True, True, inc="pe" if tc == 1 else None)
                    P.wait("scalar", "pe", mb_mm)
                    if um7_prev is not None:
                        P.wait("scalar", "dve", um7_prev)
                    mb_ev = P.op("scalar", lambda s: s.copy(MBSB[:, :], PS[:, 2048:3072]), inc="act")
                    prev_mb_ev = mb_ev
                else:
                    mb_ev = None

                silu_i = {}
                h_i = {}
                um_i = {}
                g_done_i = {}
                u_done_i = {}
                for i in range(8):
                    pb = (i % 2) * 1024
                    if i == 0:
                        P.wait("tensor", "qg", load_q[n_g])
                        if prev_acc_even is not None:
                            P.wait("tensor", "dve", prev_acc_even)
                    elif i == 1:
                        if prev_acc_odd is not None:
                            P.wait("tensor", "dve", prev_acc_odd)
                    else:
                        P.wait("tensor", "act", silu_i[i - 2])
                    for k in range(NK):
                        for tc in range(2):
                            g_done_i[i] = mm(PS[:, pb + tc * 512: pb + (tc + 1) * 512],
                                             gslot[:, k * 1024 + i * 128: k * 1024 + (i + 1) * 128],
                                             BIG1[:, k * 2048 + th + tc * 512: k * 2048 + th + (tc + 1) * 512],
                                             k == 0, k == NK - 1,
                                             inc="pe" if (k == NK - 1 and tc == 1) else None)
                    P.wait("scalar", "pe", g_done_i[i])
                    if j == 0 and ln_dve_last is not None:
                        if i == 4:
                            P.wait("scalar", "dve", ln_dve_last)
                        if i == 6:
                            P.wait("scalar", "qs", ln_store_last)
                    silu_i[i] = P.op("scalar",
                                     lambda s, i=i, pb=pb: s.activation(HBUF[:, i * 1024:(i + 1) * 1024], PS[:, pb:pb + 1024], AF.Silu),
                                     inc="act")
                cons_mark[n_g] = ("pe", g_done_i[7])
                silu7_prev = silu_i[7]
                for i in range(8):
                    pb = (i % 2) * 1024
                    if i == 0:
                        P.wait("tensor", "qg", load_q[n_u])
                        P.wait("tensor", "act", silu_i[6])
                    elif i == 1:
                        P.wait("tensor", "act", silu_i[7])
                    else:
                        P.wait("tensor", "dve", um_i[i - 2])
                    for k in range(NK):
                        for tc in range(2):
                            u_done_i[i] = mm(PS[:, pb + tc * 512: pb + (tc + 1) * 512],
                                             uslot[:, k * 1024 + i * 128: k * 1024 + (i + 1) * 128],
                                             BIG1[:, k * 2048 + th + tc * 512: k * 2048 + th + (tc + 1) * 512],
                                             k == 0, k == NK - 1,
                                             inc="pe" if (k == NK - 1 and tc == 1) else None)
                    P.wait("vector", "pe", u_done_i[i])
                    if kind == "e":
                        if i == 0:
                            P.wait("vector", "act", mb_ev)
                        um_i[i] = P.op("vector",
                                       lambda v, i=i, pb=pb: v.tensor_tensor(UM[i % 2][:, :], PS[:, pb:pb + 1024], MBSB[:, :], ALU.mult),
                                       inc="dve")
                    else:
                        um_i[i] = P.op("vector",
                                       lambda v, i=i, pb=pb: v.tensor_scalar(UM[i % 2][:, :], PS[:, pb:pb + 1024], RATIO, None, ALU.mult),
                                       inc="dve")
                    P.wait("vector", "act", silu_i[i])
                    if debug and not first_acc_written and i == 0:
                        P.wait("vector", "qg", q_dbg_ys)
                    h_i[i] = P.op("vector",
                                  lambda v, i=i: v.tensor_tensor(HBUF[:, i * 1024:(i + 1) * 1024], HBUF[:, i * 1024:(i + 1) * 1024], UM[i % 2][:, :], ALU.mult),
                                  inc="dve")
                cons_mark[n_u] = ("pe", u_done_i[7])
                um7_prev = um_i[7]

                n_d = emit_load(wsrc(kind, idx, "d"))
                dslot = SLOT[n_d % 2]

                wd_d = {}
                for d in range(NK):
                    if d == 0:
                        P.wait("tensor", "qg", load_q[n_d])
                        P.wait("tensor", "dve", h_i[6])
                    if d >= 2:
                        P.wait("tensor", "dve", acc_last[d - 2])
                    pb = (d % 2) * 1024
                    for k in range(8):
                        if d == 0 and k == 7:
                            P.wait("tensor", "dve", h_i[7])
                        for tc in range(2):
                            wd_d[d] = mm(PS[:, pb + tc * 512: pb + (tc + 1) * 512],
                                         dslot[:, k * 2048 + d * 128: k * 2048 + (d + 1) * 128],
                                         HBUF[:, k * 1024 + tc * 512: k * 1024 + (tc + 1) * 512],
                                         k == 0, k == 7,
                                         inc="pe" if (k == 7 and tc == 1) else None)
                    P.wait("vector", "pe", wd_d[d])
                    if j == 0:
                        acc_last[d] = P.op("vector",
                                           lambda v, d=d, pb=pb, th=th: v.tensor_tensor(
                                               ACC[:, d * 1024:(d + 1) * 1024],
                                               BIG1[:, d * 2048 + th: d * 2048 + th + 1024],
                                               PS[:, pb:pb + 1024], ALU.add),
                                           inc="dve")
                        first_acc_written = True
                    else:
                        acc_last[d] = P.op("vector",
                                           lambda v, d=d, pb=pb: v.tensor_tensor(
                                               ACC[:, d * 1024:(d + 1) * 1024],
                                               ACC[:, d * 1024:(d + 1) * 1024],
                                               PS[:, pb:pb + 1024], ALU.add),
                                           inc="dve")
                cons_mark[n_d] = ("pe", wd_d[NK - 1])
                prev_acc_even = acc_last[14]
                prev_acc_odd = acc_last[15]
                if debug and hf == 0 and j == len(passes) - 1:
                    P.wait("gpsimd", "dve", acc_last[NK - 1])
                    q_hard = gdma(dbg_hb, ACC[:, :])
                    P.wait("gpsimd", "qg", q_hard)

            zqf = {}
            fstat = {}
            for d in range(NK):
                P.wait("scalar", "dve", acc_last[NK - 1] if d == 0 else acc_last[d])
                if d >= 2:
                    P.wait("scalar", "pe", fstat[d - 2])
                zqf[d] = P.op("scalar",
                              lambda s, d=d: s.activation(ZSQF[d % 2][:, :], ACC[:, d * 1024:(d + 1) * 1024], AF.Square),
                              inc="act")
                if d == 0:
                    P.wait("tensor", "dve", acc_last[NK - 1])
                    if prev_mb_ev is not None:
                        P.wait("tensor", "act", prev_mb_ev)
                P.wait("tensor", "act", zqf[d])
                for tc in range(2):
                    mm(PS[0:1, 2048 + tc * 512: 2048 + (tc + 1) * 512], ONESCOLb[:, :],
                       ACC[:, d * 1024 + tc * 512: d * 1024 + (tc + 1) * 512],
                       d == 0, d == NK - 1)
                    fstat[d] = mm(PS[0:1, 3072 + tc * 512: 3072 + (tc + 1) * 512], ONESCOLb[:, :],
                                  ZSQF[d % 2][:, tc * 512:(tc + 1) * 512],
                                  d == 0, d == NK - 1,
                                  inc="pe" if tc == 1 else None)
            P.wait("vector", "pe", fstat[NK - 1])
            def dvf(fn):
                c = P.op("vector", fn, inc="dve")
                P.wait("vector", "dve", c)
                return c
            dvf(lambda v: v.tensor_scalar(FR1[:, :], PS[0:1, 2048:3072], 1.0 / D, None, ALU.mult))
            dvf(lambda v: v.tensor_scalar(FR2[:, :], PS[0:1, 3072:4096], 1.0 / D, None, ALU.mult))
            dvf(lambda v: v.tensor_tensor(PS[0:1, 2048:3072], FR1[:, :], FR1[:, :], ALU.mult))
            dvf(lambda v: v.tensor_tensor(PS[0:1, 3072:4096], FR2[:, :], PS[0:1, 2048:3072], ALU.subtract))
            f_var = dvf(lambda v: v.tensor_scalar(PS[0:1, 3072:4096], PS[0:1, 3072:4096], LN_EPS, None, ALU.add))
            P.wait("scalar", "dve", f_var)
            f_ln = P.op("scalar", lambda s: s.activation(PS[0:1, 2048:3072], PS[0:1, 3072:4096], AF.Ln), inc="act")
            P.wait("vector", "act", f_ln)
            f_lh = dvf(lambda v: v.tensor_scalar(PS[0:1, 2048:3072], PS[0:1, 2048:3072], -0.5, None, ALU.mult))
            P.wait("scalar", "dve", f_lh)
            f_rstd = P.op("scalar", lambda s: s.activation(FR2[:, :], PS[0:1, 2048:3072], AF.Exp), inc="act")
            P.wait("vector", "act", f_rstd)
            f_mr = dvf(lambda v: v.tensor_tensor(FR1[:, :], FR1[:, :], FR2[:, :], ALU.mult))
            P.wait("tensor", "dve", f_mr)
            fbc = None
            for tc in range(2):
                mm(PS[:, 2048 + tc * 512: 2048 + (tc + 1) * 512], ONES1b[:, :],
                   FR2[0:1, tc * 512:(tc + 1) * 512], True, True)
                fbc = mm(PS[:, 3072 + tc * 512: 3072 + (tc + 1) * 512], ONES1b[:, :],
                         FR1[0:1, tc * 512:(tc + 1) * 512], True, True,
                         inc="pe" if tc == 1 else None)
            P.wait("scalar", "pe", fbc)
            P.op("scalar", lambda s: s.copy(UM[0][:, :], PS[:, 2048:3072]), inc="act")
            fbc_ev = P.op("scalar", lambda s: s.copy(UM[1][:, :], PS[:, 3072:4096]), inc="act")
            st_q = {}
            out_d = {}
            P.wait("vector", "act", fbc_ev)
            for d in range(NK):
                if d >= 2:
                    P.wait("vector", "qs", st_q[d - 2])
                P.op("vector", lambda v, d=d: v.tensor_tensor(
                    T1F[:, :], ACC[:, d * 1024:(d + 1) * 1024], UM[0][:, :], ALU.mult),
                    inc="dve")
                out_d[d] = P.op("vector", lambda v, d=d: v.tensor_tensor(
                    OST[d % 2][:, :], T1F[:, :], UM[1][:, :], ALU.subtract),
                    inc="dve")
                P.wait("sync", "dve", out_d[d])
                st_q[d] = sdma(outT[d * 128:(d + 1) * 128, th:th + HFT], OST[d % 2][:, :])
            P.wait("vector", "qs", st_q[NK - 1])
            ln_dve_last = out_d[NK - 1]
            ln_store_last = st_q[NK - 1]

        P.wait("sync", "qs", P.cnt["qs"])
        if debug:
            P.wait("gpsimd", "qg", P.cnt["qg"])

        P.check()

        with nc.Block() as block:
            block.tensor(lambda t: P.replay(t, "tensor"))
            block.vector(lambda v: P.replay(v, "vector"))
            block.scalar(lambda s: P.replay(s, "scalar"))
            block.gpsimd(lambda g: P.replay(g, "gpsimd"))
            block.sync(lambda s: P.replay(s, "sync"))

    return nc


def _rearr(a, kp=128):
    K = a.shape[0] // kp
    return np.ascontiguousarray(a.reshape(K, kp, a.shape[1]).transpose(1, 0, 2).reshape(kp, K * a.shape[1]))


def kernel(L0_attn_x, L0_attn_W_in, L0_attn_W_out, L0_attn_log_A, L0_attn_W_dt,
           L0_attn_b_dt, L0_ffn_centroids, L0_ffn_gate_bias, L0_ffn_wg_s,
           L0_ffn_wu_s, L0_ffn_wd_s, L0_ffn_wg, L0_ffn_wu, L0_ffn_wd):
    f = np.float32
    x = np.asarray(L0_attn_x, f)
    W_in = np.asarray(L0_attn_W_in, f)
    W_out = np.asarray(L0_attn_W_out, f)
    log_A = np.asarray(L0_attn_log_A, f)
    W_dt = np.asarray(L0_attn_W_dt, f)
    b_dt = np.asarray(L0_attn_b_dt, f)
    cent = np.asarray(L0_ffn_centroids, f)
    gb = np.asarray(L0_ffn_gate_bias, f)
    wg_s = np.asarray(L0_ffn_wg_s, f)
    wu_s = np.asarray(L0_ffn_wu_s, f)
    wd_s = np.asarray(L0_ffn_wd_s, f)
    wg = np.asarray(L0_ffn_wg, f)
    wu = np.asarray(L0_ffn_wu, f)
    wd = np.asarray(L0_ffn_wd, f)

    winT_r = np.ascontiguousarray(
        W_in.T.reshape(16, 128, 16, 128).transpose(1, 2, 0, 3).reshape(128, 32768))
    woutT = W_out.T
    woutT_q = np.stack([_rearr(np.ascontiguousarray(woutT[:, q * 512:(q + 1) * 512]))
                        for q in range(4)])
    wdtT_r = _rearr(np.ascontiguousarray(W_dt.T))
    centT_r = _rearr(np.ascontiguousarray(cent.T))
    logA_r = np.ascontiguousarray(log_A.reshape(-1).reshape(16, 128).T)
    bdt_c = np.ascontiguousarray(b_dt.reshape(32, 1))
    gb_row = np.ascontiguousarray(gb.reshape(1, 16))
    wg_r = np.stack([_rearr(wg[e]) for e in range(NE)])
    wu_r = np.stack([_rearr(wu[e]) for e in range(NE)])
    wd_r = np.stack([_rearr(wd[e]) for e in range(NE)])
    wgs_r = np.stack([_rearr(np.ascontiguousarray(wg_s[:, s * 1024:(s + 1) * 1024])) for s in range(2)])
    wus_r = np.stack([_rearr(np.ascontiguousarray(wu_s[:, s * 1024:(s + 1) * 1024])) for s in range(2)])
    wds_r = np.stack([_rearr(np.ascontiguousarray(wd_s[s * 1024:(s + 1) * 1024, :])) for s in range(2)])
    SEL = np.kron(np.eye(32, dtype=f), np.ones((1, 64), f))
    SELB = np.kron(np.eye(16, dtype=f), np.ones((1, 128), f))
    EYE = np.eye(128, dtype=f)
    ONESCOL = np.ones((128, 1), f)
    ONES1 = np.ones((1, 128), f)

    shared = {
        "winT_r": winT_r, "woutT_q": woutT_q, "wdtT_r": wdtT_r, "centT_r": centT_r,
        "logA_r": logA_r, "bdt_c": bdt_c, "gb_row": gb_row,
        "wg_r": wg_r, "wu_r": wu_r, "wd_r": wd_r,
        "wgs_r": wgs_r, "wus_r": wus_r, "wds_r": wds_r,
        "SEL": SEL, "SELB": SELB, "EYE": EYE, "ONESCOL": ONESCOL, "ONES1": ONES1,
    }
    in_maps = []
    for c in range(8):
        m = dict(shared)
        m["xT_r"] = _rearr(np.ascontiguousarray(x[c].T))
        in_maps.append(m)

    nc = build_program(debug=DEBUG)
    res = run_bass_kernel_spmd(nc, in_maps, core_ids=list(range(8)))
    out = np.stack([np.ascontiguousarray(r["outT"].T) for r in res.results])
    if DEBUG:
        kernel._debug = res.results
    return out.astype(np.float32)
